# version 06
# baseline (speedup 1.0000x reference)
import math
import numpy as np
import ml_dtypes
from contextlib import ExitStack

import concourse.bass as bass
import concourse.mybir as mybir
from concourse.bass_utils import run_bass_kernel_spmd

F32 = mybir.dt.float32
BF16 = mybir.dt.bfloat16
I32 = mybir.dt.int32
AF = mybir.ActivationFunctionType
ALU = mybir.AluOpType
AX = mybir.AxisListType

P = 128
D = 1024
FC = 8
S = 4096
TOWN = 2048
NSLOT = 32
NCH = 8
CH = 512
H0 = 3072
H1 = 4096
NHC1 = H1 // P
NH = 16
HD = 64
DR = 32
DC = 128
RC = 12582912.0
P1B_NCH = NCH
P1B_NSL = 4
P1B_TAIL = True
P1B_OPS = 9
P1B_SUB = 9
SAFE = False
SAFE2 = False
INV_SQRT96 = 1.0 / math.sqrt(96.0)


class Prog:
    ENGINES = ("t", "v", "s", "g", "y")
    NLANE = 8

    def __init__(self, nc, ctx):
        self.nc = nc
        self.ops = {e: [] for e in self.ENGINES}
        self.sem = {}
        for name in ("t", "v", "s", "g"):
            self.sem[name] = ctx.enter_context(nc.semaphore("sem_" + name))
        for q in ("y", "g_", "s_"):
            for l in range(self.NLANE):
                name = f"d{q}{l}"
                self.sem[name] = ctx.enter_context(nc.semaphore("sem_" + name))
        self.cnt = {name: 0 for name in self.sem}
        self.waited = {e: {name: 0 for name in self.sem} for e in self.ENGINES}
        self.dma_idx = {"y": 0, "g": 0, "s": 0}

    def emit(self, eng, fn, waits=(), inc=None, inc_amt=1):
        tok = None
        if inc is not None:
            self.cnt[inc] += inc_amt
            tok = (inc, self.cnt[inc])
        real = []
        for w in waits:
            if w is None:
                continue
            sname, val = w
            if val > self.waited[eng][sname]:
                real.append((sname, val))
                self.waited[eng][sname] = val
        self.ops[eng].append((fn, tuple(real), tok))
        return tok

    def dma(self, q, out, in_, waits=(), slow=False):
        qkey = {"y": "y", "g": "g_", "s": "s_"}[q]
        i = self.dma_idx[q]
        self.dma_idx[q] = i + 1
        lane = i % self.NLANE
        semname = f"d{qkey}{lane}"
        lane_prev = (semname, self.cnt[semname])
        if slow:
            fn = lambda h, out=out, in_=in_: h.dma_start(
                out=out, in_=in_, allow_slow_non_contiguous=True)
        else:
            fn = lambda h, out=out, in_=in_: h.dma_start(out=out, in_=in_)
        return self.emit(q, fn, waits=list(waits) + [lane_prev],
                         inc=semname, inc_amt=16)

    def barrier(self):
        toks = [(n, self.cnt[n]) for n in self.sem if self.cnt[n] > 0]
        for e in self.ENGINES:
            real = []
            for (sname, val) in toks:
                if val > self.waited[e][sname]:
                    real.append((sname, val))
                    self.waited[e][sname] = val
            if real:
                self.ops[e].append((None, tuple(real), None))

    def replay(self, block):
        def run(eng, h):
            prev = {name: 0 for name in self.sem}
            for fn, waits, tok in self.ops[eng]:
                for (sname, val) in waits:
                    h.wait_ge(self.sem[sname], val)
                if fn is not None:
                    inst = fn(h)
                    if tok is not None:
                        amt = tok[1] - prev[tok[0]] if tok[0].startswith("d") else 1
                        prev[tok[0]] = tok[1]
                        inst.then_inc(self.sem[tok[0]], 16 if tok[0].startswith("d") else 1)

        @block.tensor
        def _(h):
            run("t", h)

        @block.vector
        def _(h):
            run("v", h)

        @block.scalar
        def _(h):
            run("s", h)

        @block.gpsimd
        def _(h):
            run("g", h)

        @block.sync
        def _(h):
            run("y", h)


class Arena:
    def __init__(self, ap, nbytes):
        self.ap = ap
        self.nbytes = nbytes
        self.free = [(0, nbytes)]

    def _alloc(self, nbytes, align):
        nbytes = (nbytes + align - 1) // align * align
        for i, (off, sz) in enumerate(self.free):
            aoff = (off + align - 1) // align * align
            if aoff + nbytes <= off + sz:
                repl = []
                if aoff > off:
                    repl.append((off, aoff - off))
                if off + sz > aoff + nbytes:
                    repl.append((aoff + nbytes, off + sz - (aoff + nbytes)))
                self.free[i:i + 1] = repl
                return aoff, nbytes
        raise MemoryError(f"arena OOM: want {nbytes}, free={self.free}")

    def _free(self, off, nbytes):
        self.free.append((off, nbytes))
        self.free.sort()
        merged = []
        for o, s in self.free:
            if merged and merged[-1][0] + merged[-1][1] == o:
                merged[-1] = (merged[-1][0], merged[-1][1] + s)
            else:
                merged.append([o, s])
        self.free = [tuple(m) for m in merged]


class Region:
    def __init__(self, arena, nbytes, align=64):
        self.arena = arena
        self.off, self.nb = arena._alloc(nbytes, align)

    def f32(self, rows, cols, byteoff=0):
        o = self.off + byteoff
        return self.arena.ap.bitcast(F32)[0:rows, o // 4: o // 4 + cols]

    def bf16(self, rows, cols, byteoff=0):
        o = self.off + byteoff
        return self.arena.ap.bitcast(BF16)[0:rows, o // 2: o // 2 + cols]

    def free(self):
        self.arena._free(self.off, self.nb)


def build_nc(debug=False):
    nc = bass.Bass("TRN2", target_bir_lowering=False, debug=False)

    def din(name, shape, dt=F32):
        return nc.dram_tensor(name, shape, dt, kind="ExternalInput").ap()

    x_in = din("x_in", [FC, P, S])
    l0_wq = din("l0_wq", [FC, P, HD])
    l0_wk = din("l0_wk", [FC, P, HD])
    l0_wv = din("l0_wv", [FC, P, HD])
    l0_wo = din("l0_wo", [HD, D])
    l0_wg = din("l0_wg", [FC, P, H0], I32)
    l0_wu = din("l0_wu", [FC, P, H0], I32)
    l0_wd = din("l0_wd", [H0 // P, P, D], I32)
    l1_wkv = din("l1_wkv", [FC, P, DC])
    l1_wkup = din("l1_wkup", [DC, D])
    l1_wvup = din("l1_wvup", [DC, D])
    l1_wq = din("l1_wq", [FC, P, D])
    l1_wqr1 = din("l1_wqr1", [FC, P, DR])
    l1_wqr2 = din("l1_wqr2", [FC, P, DR])
    l1_wkr1 = din("l1_wkr1", [DC, DR])
    l1_wkr2 = din("l1_wkr2", [DC, DR])
    l1_wo = din("l1_wo", [HD, NH * D])
    l1_wg = din("l1_wg", [NHC1, P, D], F32)
    l1_wu = din("l1_wu", [NHC1, P, D], F32)
    l1_wd = din("l1_wd", [NHC1, P, D], F32)
    kfmask = din("kfmask", [HD, NSLOT])
    keybias = din("keybias", [1, S])
    ut_mask = din("ut_mask", [P, P])
    dmask = din("dmask", [4, P, CH])
    cs1q = din("cs1q", [DR, TOWN])
    cs2q = din("cs2q", [DR, TOWN])
    cs1k = din("cs1k", [DR, S])
    cs2k = din("cs2k", [DR, S])
    identf = din("identf", [P, P])

    y_out = nc.dram_tensor("y", [FC, P, TOWN], F32, kind="ExternalOutput").ap()
    dbg = {}
    if debug:
        for name, shape in [("d_x1", [FC, P, S]), ("d_s0", [FC, P, S]),
                            ("d_ab", [4, P, NSLOT]), ("d_xq", [FC, P, TOWN]),
                            ("d_ckv", [P, S]), ("d_s1", [FC, P, TOWN]),
                            ("d_mw", [P, 8]), ("d_isx", [P, NSLOT // 2]),
                            ("d_num", [65, S]), ("d_rec", [1, S]),
                            ("d_qf", [64, P]), ("d_kf", [64, P]),
                            ("d_abf", [P, P]), ("d_vt", [P, 65])]:
            dbg[name] = nc.dram_tensor(name, shape, F32, kind="ExternalOutput").ap()

    s0_scr = nc.dram_tensor("s0_scr", [FC, P, S], BF16, kind="Internal").ap()
    x1_scr = nc.dram_tensor("x1_scr", [FC, P, S], F32, kind="Internal").ap()
    tern_g = nc.dram_tensor("tern_g", [NHC1, P, D], BF16, kind="Internal").ap()
    tern_u = nc.dram_tensor("tern_u", [NHC1, P, D], BF16, kind="Internal").ap()
    tern_d = nc.dram_tensor("tern_d", [NHC1, P, D], BF16, kind="Internal").ap()
    wd_bf_scr = nc.dram_tensor("wd_bf_scr", [H0 // P, P, D], BF16, kind="Internal").ap()
    attnf_scr = nc.dram_tensor("attnf_scr", [NH, HD, TOWN], BF16, kind="Internal").ap()
    drow = nc.dram_tensor("drow", [1, 16384], F32, kind="Internal").ap()

    def sbuf(name, shape, dt):
        return nc.alloc_sbuf_tensor(name, shape, dt).ap()

    wq0 = sbuf("wq0", [P, FC * HD], BF16)
    wk0 = sbuf("wk0", [P, FC * HD], BF16)
    wv0 = sbuf("wv0", [P, FC * HD], BF16)
    wo0 = sbuf("wo0", [HD, D], BF16)
    wkv1 = sbuf("wkv1", [P, FC * DC], BF16)
    wkup1 = sbuf("wkup1", [DC, D], BF16)
    wvup1 = sbuf("wvup1", [DC, D], BF16)
    wqr1 = sbuf("wqr1", [P, FC * DR], BF16)
    wqr2 = sbuf("wqr2", [P, FC * DR], BF16)
    wkr1 = sbuf("wkr1", [DC, DR], BF16)
    wkr2 = sbuf("wkr2", [DC, DR], BF16)
    kfm_sb = sbuf("kfm_sb", [HD, NSLOT], F32)
    utm_sb = sbuf("utm_sb", [P, P], F32)
    identb = sbuf("identb", [P, P], BF16)
    ident32 = sbuf("ident32", [P, P], F32)
    ones_col = sbuf("ones_col", [P, 1], F32)
    ones_row = sbuf("ones_row", [1, P], F32)
    a0_tm = sbuf("a0_tm", [P, NSLOT], F32)
    b0_tm = sbuf("b0_tm", [P, NSLOT], F32)
    a1_tm = sbuf("a1_tm", [P, NSLOT], F32)
    b1_tm = sbuf("b1_tm", [P, NSLOT], F32)
    isx_tm = sbuf("isx_tm", [P, NSLOT // 2], F32)
    mcol = sbuf("mcol", [P, 8], F32)
    rowbuf = sbuf("rowbuf", [1, 4096], F32)
    tmsc = sbuf("tmsc", [P, 64], F32)

    def row(i, n=CH):
        return rowbuf[0:1, i * CH:i * CH + n]

    AR_BYTES = (nc.sbuf_bytes_remaining - 3072) // 64 * 64
    arena_t = nc.alloc_sbuf_tensor("arena", [P, AR_BYTES], mybir.dt.int8).ap()
    A = Arena(arena_t, AR_BYTES)

    with ExitStack() as ctx:
        pr = Prog(nc, ctx)
        block = ctx.enter_context(nc.Block())
        PS = [nc.alloc_psum_tensor(f"ps{i}", [P, 512], F32).ap() for i in range(2)]
        PSW = [nc.alloc_psum_tensor(f"psw{i}", [P, 1024], F32).ap() for i in range(2)]
        PS.append(PSW[0][:, 0:512])
        PS.append(PSW[0][:, 512:1024])
        PS.append(PSW[1][:, 0:512])
        PS.append(PSW[1][:, 512:1024])
        PS.append(nc.alloc_psum_tensor("ps6", [P, 512], F32).ap())
        PSB = nc.alloc_psum_tensor("psb7", [P, 1024], BF16).ap()
        PS.append(None)

        waccr = Region(A, 2 * NHC1 * 4)
        wdacc_r = Region(A, NHC1 * 4)
        bg = [[] for _ in range(8)]
        rowg = [[] for _ in range(8)]

        def TT(eng, out, in0, in1, op, waits=()):
            return pr.emit(eng, lambda h: h.tensor_tensor(out, in0, in1, op),
                           waits=waits, inc=eng)

        def MM(out, lhsT, rhs, start=True, stop=True, waits=(), tile_position=None):
            def fn(h, out=out, lhsT=lhsT, rhs=rhs, start=start, stop=stop,
                   tp=tile_position):
                if tp is not None:
                    return h.matmul(out, lhsT, rhs, start=start, stop=stop,
                                    skip_group_check=True, tile_position=tp)
                return h.matmul(out, lhsT, rhs, start=start, stop=stop,
                                skip_group_check=True)
            return pr.emit("t", fn, waits=waits, inc="t")

        drow_off = [0]

        def _dslot(n):
            if drow_off[0] + n > 16384:
                drow_off[0] = 0
            off = drow_off[0]
            drow_off[0] += n
            return drow[0:1, off:off + n]

        def tm_to_row(tm_ap, w, slot, waits=()):
            d = _dslot(P * w)
            t1 = pr.dma("g", d.rearrange("a (j p) -> (a p) j", p=P), tm_ap,
                        waits=waits, slow=True)
            t2 = pr.dma("g", row(slot, P * w), d, waits=[t1] + rowg[slot])
            rowg[slot] = []
            return t2

        def row_to_tm(row_ap, w, dst_tm, waits=()):
            d = _dslot(P * w)
            t1 = pr.dma("g", d, row_ap, waits=waits)
            return pr.dma("g", dst_tm, d.rearrange("a (j p) -> (a p) j", p=P),
                          waits=[t1], slow=True)

        def bcast(bank, slot, n, waits=(), rowap=None):
            src = rowap if rowap is not None else row(slot, n)
            tok = MM(PS[bank][0:P, 0:n], ones_row[:], src,
                     waits=list(waits) + bg[bank])
            bg[bank] = []
            if rowap is None:
                rowg[slot].append(tok)
            return tok

        def tm_to_row_pe(tm_ap, w, slot, bank, waits=()):
            mm = None
            for k in range(w):
                mm = MM(PS[bank][0:1, k * P:(k + 1) * P], tm_ap[:, k:k + 1],
                        ident32[:], start=(k == 0), stop=(k == w - 1),
                        waits=(list(waits) + bg[bank]) if k == 0 else [])
                if k == 0:
                    bg[bank] = []
            cp = pr.emit("v", lambda h: h.tensor_copy(
                row(slot, P * w), PS[bank][0:1, 0:P * w]),
                waits=[mm] + rowg[slot], inc="v")
            rowg[slot] = []
            bg[bank] = [cp]
            return cp

        wt = []
        for dst, src in ((wq0, l0_wq), (wk0, l0_wk), (wv0, l0_wv),
                         (wkv1, l1_wkv), (wqr1, l1_wqr1), (wqr2, l1_wqr2)):
            wt.append(pr.dma("g", dst.rearrange("p (c k) -> p c k", c=FC),
                             src.transpose([1, 0, 2])))
        wt.append(pr.dma("g", wo0[:], l0_wo[:]))
        wt.append(pr.dma("g", wkup1[:], l1_wkup[:]))
        wt.append(pr.dma("g", wvup1[:], l1_wvup[:]))
        wt.append(pr.dma("g", wkr1[:], l1_wkr1[:]))
        wt.append(pr.dma("g", wkr2[:], l1_wkr2[:]))
        wt.append(pr.dma("y", kfm_sb[:], kfmask[:]))
        wt.append(pr.dma("y", utm_sb[:], ut_mask[:]))
        wt.append(pr.dma("g", identb[:], identf[:]))
        wt.append(pr.dma("y", ident32[:], identf[:]))
        m1 = pr.emit("v", lambda h: h.memset(ones_col[:], 1.0), inc="v")
        m2 = pr.emit("v", lambda h: h.memset(ones_row[:], 1.0), inc="v")
        consts = [m1, m2]
        for w in (wqr1, wqr2):
            consts.append(pr.emit(
                "v", lambda h, w=w: h.tensor_scalar_mul(w[:], w[:], INV_SQRT96),
                waits=wt, inc="v"))
        wt_all = wt + consts

        affine_prev = [None]

        def ln_affine_chunk(sb, sb2, mmt, mmt2, a_tm, b_tm, cc):
            c1 = pr.emit("v", lambda h, sb=sb: h.tensor_copy(
                row(0), PS[sb][0:1, 0:CH]), waits=[mmt, affine_prev[0]] + rowg[0],
                inc="v")
            rowg[0] = []
            c2 = pr.emit("v", lambda h, sb2=sb2: h.tensor_copy(
                row(1), PS[sb2][0:1, 0:CH]), waits=[mmt2] + rowg[1], inc="v")
            rowg[1] = []
            bg[sb].append(c1)
            bg[sb2].append(c2)
            s_tm = tmsc[:, 0:4]
            s2_tm = tmsc[:, 4:8]
            m_tm = tmsc[:, 8:12]
            v_tm = tmsc[:, 12:16]
            mm = None
            for k in range(8):
                srcr = row(0) if k < 4 else row(1)
                kk = k % 4
                mm = MM(PS[5][0:P, k:k + 1], srcr[0:1, kk * P:(kk + 1) * P],
                        ones_row[0:1, 0:1], start=(k == 0), stop=(k == 7),
                        waits=([c1, c2] + bg[5]) if k == 0 else [])
                if k == 0:
                    bg[5] = []
            t1 = pr.emit("v", lambda h: h.tensor_copy(
                tmsc[:, 0:8], PS[5][0:P, 0:8]), waits=[mm], inc="v")
            t2 = t1
            bg[5] = [t1]
            rowg[0].append(mm)
            rowg[1].append(mm)
            aa = a_tm[:, 4 * cc:4 * cc + 4]
            bb = b_tm[:, 4 * cc:4 * cc + 4]
            e1 = pr.emit("v", lambda h: h.tensor_scalar_mul(m_tm, s_tm, 1.0 / D),
                         waits=[t1], inc="v")
            e2 = pr.emit("v", lambda h: h.tensor_scalar_mul(s2_tm, s2_tm, 1.0 / D),
                         waits=[t2], inc="v")
            e3 = TT("v", v_tm, m_tm, m_tm, ALU.mult, waits=[e1])
            e4 = TT("v", v_tm, s2_tm, v_tm, ALU.subtract, waits=[e2, e3])
            e5 = pr.emit("v", lambda h: h.tensor_scalar_add(v_tm, v_tm, 1e-5),
                         waits=[e4], inc="v")
            e6 = pr.emit("s", lambda h: h.activation(v_tm, v_tm, AF.Sqrt),
                         waits=[e5], inc="s")
            e7 = pr.emit("v", lambda h: h.reciprocal(aa, v_tm), waits=[e6], inc="v")
            e8 = pr.emit("v", lambda h: h.scalar_tensor_tensor(
                bb, m_tm, -1.0, aa, ALU.mult, ALU.mult), waits=[e7], inc="v")
            affine_prev[0] = e8
            return e8

        sq_carry = [None]

        def ln_stats_chunk(buf, sq, a_tm, b_tm, cc, waits):
            mmt = mmt2 = None
            prev_sq = sq_carry
            for fc in range(FC):
                sqt = pr.emit("s", lambda h, fc=fc: h.activation(
                    sq, buf[:, fc * CH:(fc + 1) * CH], AF.Square),
                    waits=list(waits) + [prev_sq[0]], inc="s")
                mmt = MM(PS[6][0:1, 0:CH], ones_col[:], buf[:, fc * CH:(fc + 1) * CH],
                         start=(fc == 0), stop=(fc == FC - 1),
                         waits=list(waits) + (bg[6] if fc == 0 else []) + [consts[0]])
                if fc == 0:
                    bg[6] = []
                mmt2 = MM(PS[4][0:1, 0:CH], ones_col[:], sq,
                          start=(fc == 0), stop=(fc == FC - 1),
                          waits=[sqt] + (bg[4] if fc == 0 else []))
                if fc == 0:
                    bg[4] = []
                prev_sq[0] = mmt2
            return ln_affine_chunk(6, 4, mmt, mmt2, a_tm, b_tm, cc)

        xr = Region(A, 2 * FC * CH * 4)
        sqA_r = Region(A, CH * 4)
        sqA = sqA_r.f32(P, CH)
        xuse = [None, None]
        for cc in range(NCH):
            buf = xr.f32(P, FC * CH, byteoff=(cc % 2) * FC * CH * 4)
            lt = []
            for fc in range(FC):
                w = [xuse[cc % 2]] if xuse[cc % 2] else []
                lt.append(pr.dma("y", buf[:, fc * CH:(fc + 1) * CH],
                                 x_in[fc, :, cc * CH:(cc + 1) * CH], waits=w))
            xuse[cc % 2] = ln_stats_chunk(buf, sqA, a0_tm, b0_tm, cc, lt)


        wg_r = Region(A, FC * H0 * 2)
        wg_sb = wg_r.bf16(P, FC * H0)
        wgt_toks = []
        wbr = Region(A, 4 * D * 4)
        wacc = {m: waccr.f32(P, NHC1, byteoff=i * NHC1 * 4)
                for i, m in enumerate(("g", "u"))}
        wsum_done = {}
        prev_rt = [None, None, None, None]
        wdc_r = Region(A, 2 * D * 2)
        wdcvt_toks = []
        wdc_use = [[], []]
        p1w_jobs = [("G", fc) for fc in range(FC)]
        p1w_jobs += [("w", mi, m, src, hc)
                     for mi, (m, src) in enumerate((("g", l1_wg), ("u", l1_wu)))
                     for hc in range(NHC1)]
        p1w_jobs += [("c", hc) for hc in range(H0 // P)]
        p1w_pend = []
        p1w_pos = [0]

        def p1w_step(ndma):
            for fn in p1w_pend:
                fn()
            p1w_pend.clear()
            for _ in range(ndma):
                if p1w_pos[0] >= len(p1w_jobs):
                    return
                job = p1w_jobs[p1w_pos[0]]
                p1w_pos[0] += 1
                if job[0] == "G":
                    _, fc = job
                    wgt_toks.append(pr.dma(
                        "g", wg_sb[:, fc * H0:(fc + 1) * H0], l0_wg[fc]))
                elif job[0] == "w":
                    _, mi, m, src, hc = job
                    j = mi * NHC1 + hc
                    buf = wbr.f32(P, D, byteoff=(j % 4) * D * 4)
                    wtk = pr.dma("s", buf, src[hc, :, :],
                                 waits=[prev_rt[j % 4]] if prev_rt[j % 4] else [])

                    def red(buf=buf, m=m, hc=hc, j=j, wtk=wtk):
                        prev_rt[j % 4] = pr.emit(
                            "v", lambda h: h.reduce_sum(
                                wacc[m][:, hc:hc + 1], buf, axis=AX.X,
                                apply_absolute_value=True),
                            waits=[wtk], inc="v")
                        if hc == NHC1 - 1:
                            wsum_done[m] = prev_rt[j % 4]
                    p1w_pend.append(red)
                else:
                    _, hc = job
                    b_ = wdc_r.bf16(P, D, byteoff=(hc % 2) * D * 2)
                    t1_ = pr.dma("g", b_, l0_wd[hc], waits=wdc_use[hc % 2])
                    t2_ = pr.dma("g", wd_bf_scr[hc], b_, waits=[t1_])
                    wdc_use[hc % 2] = [t2_]
                    wdcvt_toks.append(t2_)

        def p1w_flush():
            while p1w_pos[0] < len(p1w_jobs) or p1w_pend:
                p1w_step(4)

        num_r = Region(A, 4 * S + 64)
        num_sb = num_r.f32(65, S)
        outn_r = Region(A, 2 * S)
        outn_sb = outn_r.bf16(64, S)
        sm = Region(A, 34 * 1024)
        tscr = sm.f32(P, CH, byteoff=0)
        o1 = CH * 4
        qrel = sm.f32(64, P, byteoff=o1)
        qmin = sm.f32(64, P, byteoff=o1 + 512)
        qexp = sm.f32(64, P, byteoff=o1 + 1024)
        krel = sm.f32(64, P, byteoff=o1 + 1536)
        kmin = sm.f32(64, P, byteoff=o1 + 2048)
        kexp = sm.f32(64, P, byteoff=o1 + 2560)
        ksum = sm.f32(64, P, byteoff=o1 + 3072)
        qf_bf = sm.bf16(64, P, byteoff=o1 + 3584)
        kf_bf = sm.bf16(64, P, byteoff=o1 + 3840)
        v_bf = sm.bf16(64, P, byteoff=o1 + 4352)
        kT_sb = sm.bf16(P, HD, byteoff=o1 + 4864)
        vT1_sb = sm.bf16(P, 96, byteoff=o1 + 5376)
        a_bf = sm.bf16(P, P, byteoff=o1 + 5888)
        state_bf = sm.bf16(64, 96, byteoff=o1 + 6400)
        state_f32 = sm.f32(64, 96, byteoff=o1 + 6656)
        abuf = sm.f32(P, CH, byteoff=o1 + 8192)
        bbuf = sm.f32(P, CH, byteoff=o1 + 8192 + 2048)
        xnbs = sm.bf16(P, FC * CH, byteoff=o1 + 8192 + 4096)
        rb_sb = sm.f32(64, CH, byteoff=o1 + 8192 + 4096 + FC * CH * 2)

        mz1 = pr.emit("v", lambda h: h.memset(state_bf, 0.0), inc="v")
        mz1b = pr.emit("v", lambda h: h.memset(state_f32, 0.0), inc="v")
        mz1c = pr.emit("v", lambda h: h.memset(vT1_sb[:, HD + 1:96], 0.0), inc="v")
        elu_prev = [None, None]
        mz2 = pr.emit("v", lambda h: h.memset(vT1_sb[:, HD:HD + 1], 1.0), inc="v")

        prev_snap = mz1
        numcp_last = None
        abuf_use = []
        bbuf_use = []
        xnbs_use = []
        elu_use = []
        tr_use = []
        a_use = []
        n_use = []
        st_use = []
        small_use = []
        for cc in range(P1B_NCH):
            buf = xr.f32(P, FC * CH, byteoff=(cc % 2) * FC * CH * 4)
            lt = []
            for fc in range(FC):
                w = [xuse[cc % 2]] if xuse[cc % 2] else []
                lt.append(pr.dma("y", buf[:, fc * CH:(fc + 1) * CH],
                                 x_in[fc, :, cc * CH:(cc + 1) * CH], waits=w))
            ra = tm_to_row_pe(a0_tm[:, 4 * cc:4 * cc + 4], 4, 0, 5,
                              waits=[affine_prev[0]])
            rb = tm_to_row_pe(b0_tm[:, 4 * cc:4 * cc + 4], 4, 1, 5,
                              waits=[affine_prev[0]])
            ba = bcast(5, 0, CH, waits=[ra])
            ca = pr.emit("v", lambda h: h.tensor_copy(abuf, PS[5][0:P, 0:CH]),
                         waits=[ba] + abuf_use, inc="v")
            abuf_use = []
            bg[5].append(ca)
            bb_ = bcast(5, 1, CH, waits=[rb])
            cb = pr.emit("v", lambda h: h.tensor_copy(bbuf, PS[5][0:P, 0:CH]),
                         waits=[bb_] + bbuf_use, inc="v")
            bbuf_use = []
            bg[5].append(cb)

            lnt_c = []
            for fc in range(FC):
                t1 = TT("v", tscr, buf[:, fc * CH:(fc + 1) * CH], abuf,
                        ALU.mult, waits=[ca, lt[fc]] + xnbs_use)
                t2 = TT("v", xnbs[:, fc * CH:(fc + 1) * CH], tscr, bbuf,
                        ALU.add, waits=[cb, t1])
                lnt_c.append(t2)
            xnbs_use = []
            for sl in range(P1B_NSL):
                js = 4 * cc + sl
                if P1B_OPS < 2:
                    prev_snap = lnt_c[-1]
                    continue
                mmq = mmk = mmv = None
                for fc in range(FC):
                    xs = xnbs[:, fc * CH + sl * P: fc * CH + sl * P + P]
                    w0 = [lnt_c[fc]] + wt_all + (bg[0] if fc == 0 else [])
                    mmq = MM(PS[0][0:64, 0:P], wq0[:, fc * HD:(fc + 1) * HD], xs,
                             start=(fc == 0), stop=(fc == FC - 1), waits=w0)
                    mmk = MM(PS[0][0:64, P:2 * P], wk0[:, fc * HD:(fc + 1) * HD],
                             xs, start=False, stop=(fc == FC - 1))
                    mmv = MM(PS[0][0:64, 2 * P:3 * P], wv0[:, fc * HD:(fc + 1) * HD],
                             xs, start=False, stop=(fc == FC - 1))
                bg[0] = []
                if P1B_OPS < 3:
                    prev_snap = mmv
                    bg[0] = [mmv]
                    continue
                e1 = pr.emit("v", lambda h: h.tensor_scalar_max(
                    qrel, PS[0][0:64, 0:P], 0.0), waits=[mmv] + small_use,
                    inc="v")
                small_use = []
                e2 = pr.emit("v", lambda h: h.tensor_scalar_min(
                    qmin, PS[0][0:64, 0:P], 0.0), waits=[mmv, elu_prev[0]],
                    inc="v")
                e3 = pr.emit("s", lambda h: h.activation(qexp, qmin, AF.Exp),
                             waits=[e2], inc="s")
                e4 = TT("v", qf_bf, qrel, qexp, ALU.add, waits=[e1, e3])
                f1 = pr.emit("v", lambda h: h.tensor_scalar_max(
                    krel, PS[0][0:64, P:2 * P], 0.0), waits=[mmv], inc="v")
                f2 = pr.emit("v", lambda h: h.tensor_scalar_min(
                    kmin, PS[0][0:64, P:2 * P], 0.0), waits=[mmv, elu_prev[1]],
                    inc="v")
                f3 = pr.emit("s", lambda h: h.activation(kexp, kmin, AF.Exp),
                             waits=[f2], inc="s")
                elu_prev[0] = e3
                elu_prev[1] = f3
                f4 = TT("v", ksum, krel, kexp, ALU.add, waits=[f1, f3])
                f5 = pr.emit("v", lambda h, js=js: h.tensor_scalar_mul(
                    kf_bf, ksum, kfm_sb[:, js:js + 1]), waits=[f4], inc="v")
                g1 = pr.emit("v", lambda h: h.tensor_copy(
                    v_bf, PS[0][0:64, 2 * P:3 * P]), waits=[mmv], inc="v")
                bg[0] = [g1]
                if P1B_OPS < 4:
                    prev_snap = g1
                    continue
                trk = pr.emit("t", lambda h: h.transpose(
                    PSB[:, 0:HD], kf_bf, identb[0:64, 0:64]),
                    waits=[f5] + bg[7], inc="t")
                bg[7] = []
                trv = pr.emit("t", lambda h: h.transpose(
                    PSB[:, HD:2 * HD], v_bf, identb[0:64, 0:64]),
                    waits=[g1], inc="t")
                c1 = pr.emit("v", lambda h: h.tensor_copy(kT_sb, PSB[:, 0:HD]),
                             waits=[trv], inc="v")
                c2 = pr.emit("v", lambda h: h.tensor_copy(
                    vT1_sb[:, 0:HD], PSB[:, HD:2 * HD]),
                    waits=[trv, mz2, mz1c], inc="v")
                bg[7] = [c1, c2]
                if P1B_OPS < 5:
                    prev_snap = c2
                    continue
                at = MM(PS[2][0:P, 0:P], kf_bf, qf_bf,
                        waits=[f5, e4] + bg[2])
                bg[2] = []
                am = TT("v", a_bf, PS[2][0:P, 0:P], utm_sb[:], ALU.mult,
                        waits=[at])
                bg[2] = [am]
                if P1B_OPS < 6:
                    prev_snap = am
                    continue
                n1 = MM(PS[3][0:96, 0:P], vT1_sb, a_bf, start=True, stop=False,
                        waits=[am, c2] + bg[3])
                bg[3] = []
                n2 = MM(PS[3][0:96, 0:P], state_bf, qf_bf, start=False, stop=True,
                        waits=[prev_snap, e4])
                ncp = pr.emit("v", lambda h, js=js: h.tensor_copy(
                    num_sb[:, js * P:(js + 1) * P], PS[3][0:65, 0:P]),
                    waits=[n2], inc="v")
                bg[3] = [ncp]
                numcp_last = ncp
                if P1B_OPS < 7:
                    prev_snap = ncp
                    continue
                su = MM(PS[4][0:64, 0:96], kT_sb, vT1_sb,
                        start=True, stop=True,
                        waits=[c1, c2] + bg[4])
                bg[4] = []
                sacc = pr.emit("v", lambda h: h.tensor_tensor(
                    state_f32, state_f32, PS[4][0:64, 0:96], ALU.add),
                    waits=[su, mz1b, prev_snap], inc="v")
                bg[4] = [sacc]
                snap = pr.emit("v", lambda h: h.tensor_copy(
                    state_bf, state_f32), waits=[sacc], inc="v")
                prev_snap = snap
                if debug and js == 0:
                    pr.dma("g", dbg["d_qf"], qf_bf, waits=[e4])
                    pr.dma("g", dbg["d_kf"], kf_bf, waits=[f5])
                    pr.dma("g", dbg["d_abf"], a_bf, waits=[am])
                    pr.dma("g", dbg["d_vt"], vT1_sb[:, 0:65], waits=[c2])
                    pr.barrier()
                small_use = [n1, n2, su, at, trk, trv, mmq]
                if sl == P1B_NSL - 1:
                    xnbs_use = [mmv]
                p1w_step(3)
                if SAFE:
                    pr.barrier()
            xuse[cc % 2] = lnt_c[-1]
            abuf_use = [lnt_c[-1]]
            bbuf_use = [lnt_c[-1]]

        p1w_flush()
        wdcvt_done = wdcvt_toks[-8:]

        sstore = [None]
        sst_r = Region(A, 2 * CH * 2)
        if not P1B_TAIL:
            pr.barrier()
        if P1B_TAIL:
            dn = rowbuf[0:1, 0:4096]
            dadd = pr.emit("v", lambda h: h.tensor_scalar_add(
                dn, num_sb[64:65, :], 1e-6),
                waits=[numcp_last] + rowg[0] + rowg[1] + rowg[2] + rowg[3] +
                      rowg[4] + rowg[5] + rowg[6] + rowg[7], inc="v")
            for i in range(8):
                rowg[i] = []
            drec = pr.emit("v", lambda h: h.reciprocal(dn, dn), waits=[dadd], inc="v")
            if debug:
                pr.dma("g", dbg["d_num"], num_sb[0:65, 0:S], waits=[numcp_last])
                pr.dma("g", dbg["d_rec"], dn, waits=[drec])
                pr.barrier()
            sstore.clear()
            on_toks = []
            rb_use = []
            for cc in range(NCH):
                bb2 = bcast(5, 0, CH, waits=[drec],
                            rowap=dn[0:1, cc * CH:(cc + 1) * CH])
                rowg[cc % 8].append(bb2) if False else None
                cb2 = pr.emit("v", lambda h: h.tensor_copy(rb_sb, PS[5][0:64, 0:CH]),
                              waits=[bb2] + rb_use, inc="v")
                bg[5].append(cb2)
                on = TT("v", outn_sb[0:64, cc * CH:(cc + 1) * CH],
                        num_sb[0:64, cc * CH:(cc + 1) * CH], rb_sb, ALU.mult,
                        waits=[cb2])
                rb_use = [on]
                on_toks.append(on)
            ev_prev = [None, None]
            for oc in range(FC):
                for cc in range(NCH):
                    bank = 6 if cc % 2 == 0 else 2
                    mo = MM(PS[bank][0:P, 0:CH], wo0[:, oc * P:(oc + 1) * P],
                            outn_sb[0:64, cc * CH:(cc + 1) * CH],
                            waits=[on_toks[cc]] + bg[bank])
                    bg[bank] = []
                    stg = sst_r.bf16(P, CH, byteoff=(cc % 2) * CH * 2)
                    ev = pr.emit("v", lambda h, stg=stg, bank=bank: h.tensor_copy(
                        stg, PS[bank][0:P, 0:CH]),
                        waits=[mo] + (ev_prev[cc % 2] or []), inc="v")
                    bg[bank] = [ev]
                    st = pr.dma("y", s0_scr[oc, :, cc * CH:(cc + 1) * CH], stg,
                                waits=[ev])
                    sstore.append(st)
                    if debug:
                        st2 = pr.dma("g", dbg["d_s0"][oc, :, cc * CH:(cc + 1) * CH],
                                     stg, waits=[ev])
                        sstore.append(st2)
                        ev_prev[cc % 2] = [st, st2]
                    else:
                        ev_prev[cc % 2] = [st, None]

        pr.barrier()
        num_r.free()
        outn_r.free()
        sm.free()
        sst_r.free()
        xr.free()
        sqA_r.free()
        wbr.free()
        wdc_r.free()

        wu_r = Region(A, FC * H0 * 2)
        wu_sb = wu_r.bf16(P, FC * H0)
        wut = [pr.dma("g", wu_sb[:, fc * H0:(fc + 1) * H0], l0_wu[fc])
               for fc in range(FC)]
        wgt = wgt_toks

        x2_r = Region(A, FC * CH * 4)
        s0_r = Region(A, FC * CH * 2)
        h_r = Region(A, (H0 // P) * CH * 2)
        wd_r = Region(A, 2 * CH * 2)
        p2s = Region(A, FC * CH * 2)
        p2sa = Region(A, 8 * 1024)
        p2sb = Region(A, 8 * 1024)
        xnb2 = p2s.bf16(P, FC * CH, byteoff=0)
        abuf2 = p2sa.f32(P, CH, byteoff=0)
        bbuf2 = p2sa.f32(P, CH, byteoff=2048)
        sgbuf = p2sa.f32(P, CH, byteoff=4096)
        ggbuf = p2sa.f32(P, CH, byteoff=6144)
        s0f = p2sb.f32(P, CH, byteoff=0)
        sq2a = p2sb.f32(P, CH, byteoff=2048)
        tres = p2sb.f32(P, CH, byteoff=4096)
        tscr2 = p2sb.f32(P, CH, byteoff=6144)

        xbuf = x2_r.f32(P, FC * CH)
        sbuf0 = s0_r.bf16(P, FC * CH)
        x2use = []
        s0use = []
        wd_use = [[], []]
        ab2_use = []
        x1st_all = []
        hbf_prev = {}
        for cc in range(NCH):
            lt = []
            for fc in range(FC):
                lt.append(pr.dma("y", xbuf[:, fc * CH:(fc + 1) * CH],
                                 x_in[fc, :, cc * CH:(cc + 1) * CH],
                                 waits=x2use))
            x2use = []
            st0 = []
            for fc in range(FC):
                st0.append(pr.dma("s", sbuf0[:, fc * CH:(fc + 1) * CH],
                                  s0_scr[fc, :, cc * CH:(cc + 1) * CH],
                                  waits=s0use + sstore[-16:]))
            s0use = []
            ra = tm_to_row_pe(a0_tm[:, 4 * cc:4 * cc + 4], 4, 0, 5, waits=[])
            rb = tm_to_row_pe(b0_tm[:, 4 * cc:4 * cc + 4], 4, 1, 5, waits=[])
            ba = bcast(5, 0, CH, waits=[ra])
            ca = pr.emit("v", lambda h: h.tensor_copy(abuf2, PS[5][0:P, 0:CH]),
                         waits=[ba] + ab2_use, inc="v")
            bg[5].append(ca)
            bb_ = bcast(5, 1, CH, waits=[rb])
            cb = pr.emit("v", lambda h: h.tensor_copy(bbuf2, PS[5][0:P, 0:CH]),
                         waits=[bb_], inc="v")
            bg[5].append(cb)
            lnt = []
            prev_t2 = None
            for fc in range(FC):
                xsl = xbuf[:, fc * CH:(fc + 1) * CH]
                t1 = TT("v", tscr2, xsl, abuf2, ALU.mult,
                        waits=[ca, prev_t2] + lt)
                t2 = TT("v", xnb2[:, fc * CH:(fc + 1) * CH], tscr2, bbuf2,
                        ALU.add, waits=[cb, t1])
                prev_t2 = t2
                lnt.append(t2)
            ab2_use = [lnt[-1]]
            hw = lnt + wgt
            h_toks = []
            for hc in range(H0 // P):
                bkg, bku = hc % 2, 2 + hc % 2
                mmg = mmu = None
                for fc in range(FC):
                    xs = xnb2[:, fc * CH:(fc + 1) * CH]
                    wgT = wg_sb[:, fc * H0 + hc * P: fc * H0 + (hc + 1) * P]
                    wuT = wu_sb[:, fc * H0 + hc * P: fc * H0 + (hc + 1) * P]
                    mmg = MM(PS[bkg][0:P, 0:CH], wgT, xs, start=(fc == 0),
                             stop=(fc == FC - 1),
                             waits=hw + (bg[bkg] if fc == 0 else []))
                    if fc == 0:
                        bg[bkg] = []
                    mmu = MM(PS[bku][0:P, 0:CH], wuT, xs, start=(fc == 0),
                             stop=(fc == FC - 1),
                             waits=[wut[fc]] + (bg[bku] if fc == 0 else []))
                    if fc == 0:
                        bg[bku] = []
                sg = pr.emit("s", lambda h, bkg=bkg: h.activation(
                    sgbuf, PS[bkg][0:P, 0:CH], AF.Silu, scale=1.0 / 127.0),
                    waits=[mmg, hbf_prev.get("sg")], inc="s")
                bg[bkg] = [sg]
                hbf = TT("v", h_r.bf16(P, CH, byteoff=hc * CH * 2), sgbuf,
                         PS[bku][0:P, 0:CH], ALU.mult,
                         waits=[sg, mmu] + [hbf_prev.get(hc)])
                bg[bku] = [hbf]
                hbf_prev["sg"] = hbf
                h_toks.append(hbf)
                if SAFE2:
                    pr.barrier()
            x1sq_mm = x1sq_mm2 = None
            x1_st = []
            prev_sq2 = sq_carry
            gg_prev2 = [None]
            dgrp_last = {}
            for grp in range(2):
                for hc in range(H0 // P):
                    wdb = wd_r.bf16(P, CH, byteoff=(hc % 2) * CH * 2)
                    wdt = pr.dma("y", wdb,
                                 wd_bf_scr[hc, :, grp * CH:(grp + 1) * CH],
                                 waits=wd_use[hc % 2] + wdcvt_done)
                    wd_use[hc % 2] = []
                    for j in range(4):
                        oc = grp * 4 + j
                        mm_last = MM(PS[j][0:P, 0:CH], wdb[:, j * P:(j + 1) * P],
                                     h_r.bf16(P, CH, byteoff=hc * CH * 2),
                                     start=(hc == 0), stop=(hc == H0 // P - 1),
                                     waits=[wdt, h_toks[hc]] +
                                           (bg[j] if hc == 0 else []))
                        if hc == 0:
                            bg[j] = []
                        dgrp_last[oc] = mm_last
                    wd_use[hc % 2] = [mm_last]
                    if grp == 1:
                        hbf_prev[hc] = mm_last
                    if SAFE2:
                        pr.barrier()
                for j in range(4):
                    oc = grp * 4 + j
                    gg = pr.emit("s", lambda h, j=j: h.activation(
                        ggbuf, PS[j][0:P, 0:CH], AF.Silu, scale=1.0 / 16129.0),
                        waits=[dgrp_last[oc], gg_prev2[0]], inc="s")
                    bg[j] = [gg]
                    s0c = pr.emit("v", lambda h, oc=oc: h.tensor_copy(
                        s0f, sbuf0[:, oc * CH:(oc + 1) * CH]), waits=st0 + [gg],
                        inc="v")
                    t1 = TT("v", tres, s0f, ggbuf, ALU.mult, waits=[s0c])
                    gg_prev2[0] = t1
                    x1sl = xbuf[:, oc * CH:(oc + 1) * CH]
                    t2 = TT("v", x1sl, x1sl, tres, ALU.add, waits=[t1])
                    stx = pr.dma("y", x1_scr[oc, :, cc * CH:(cc + 1) * CH],
                                 x1sl, waits=[t2])
                    x1_st.append(stx)
                    if debug:
                        x1_st.append(pr.dma(
                            "y", dbg["d_x1"][oc, :, cc * CH:(cc + 1) * CH],
                            x1sl, waits=[t2]))
                    sqt = pr.emit("s", lambda h, x1sl=x1sl: h.activation(
                        sq2a, x1sl, AF.Square), waits=[t2, prev_sq2[0]], inc="s")
                    x1sq_mm = MM(PS[6][0:1, 0:CH], ones_col[:], x1sl,
                                 start=(oc == 0), stop=(oc == FC - 1),
                                 waits=[t2] + (bg[6] if oc == 0 else []))
                    if oc == 0:
                        bg[6] = []
                    x1sq_mm2 = MM(PS[4][0:1, 0:CH], ones_col[:], sq2a,
                                  start=(oc == 0), stop=(oc == FC - 1),
                                  waits=[sqt] + (bg[4] if oc == 0 else []))
                    if oc == 0:
                        bg[4] = []
                    prev_sq2[0] = x1sq_mm2
                    if SAFE2:
                        pr.barrier()
            lnu = ln_affine_chunk(6, 4, x1sq_mm, x1sq_mm2, a1_tm, b1_tm, cc)
            if SAFE:
                pr.barrier()
            x2use = [lnu] + x1_st
            s0use = [t1]
            x1st_all = x1_st

        wg_sum = pr.emit("v", lambda h: h.reduce_sum(
            tmsc[:, 16:17], wacc["g"], axis=AX.X), waits=[wsum_done["g"]], inc="v")
        wu_sum = pr.emit("v", lambda h: h.reduce_sum(
            tmsc[:, 17:18], wacc["u"], axis=AX.X), waits=[wsum_done["u"]], inc="v")
        mm_g = MM(PS[5][0:1, 0:1], ones_col[:], tmsc[:, 16:17],
                  waits=[wg_sum] + bg[5])
        bg[5] = []
        mm_u = MM(PS[5][0:1, 1:2], ones_col[:], tmsc[:, 17:18], waits=[wu_sum])
        cpm = pr.emit("v", lambda h: h.tensor_copy(
            rowbuf[0:1, 0:2], PS[5][0:1, 0:2]),
            waits=[mm_g, mm_u] + rowg[0], inc="v")
        rowg[0] = []
        bg[5] = [cpm]
        e1_ = pr.emit("v", lambda h: h.tensor_scalar(
            rowbuf[0:1, 0:2], rowbuf[0:1, 0:2], 1.0 / (H1 * D), 1e-5,
            ALU.mult, ALU.max), waits=[cpm], inc="v")
        e2_ = pr.emit("v", lambda h: h.reciprocal(
            rowbuf[0:1, 3:5], rowbuf[0:1, 0:2]), waits=[e1_], inc="v")
        bcm = bcast(0, 0, 8, waits=[e2_], rowap=rowbuf[0:1, 0:8])
        rowg[0].append(bcm)
        cpm2 = pr.emit("v", lambda h: h.tensor_copy(mcol[:, 0:8], PS[0][0:P, 0:8]),
                       waits=[bcm], inc="v")
        bg[0] = [cpm2]

        pr.barrier()
        wg_r.free()
        wu_r.free()
        x2_r.free()
        s0_r.free()
        h_r.free()
        wd_r.free()
        p2s.free()
        p2sa.free()
        p2sb.free()
        waccr.free()

        xnb_own_r = Region(A, FC * TOWN * 2)
        xq_own_r = Region(A, FC * TOWN * 2)
        ckv_r = Region(A, S * 2)
        ksh_r = [Region(A, S * 2), Region(A, S * 2)]
        qsh_r = [Region(A, TOWN * 2), Region(A, TOWN * 2)]
        xnb_own = xnb_own_r.bf16(P, FC * TOWN)
        xq_own = xq_own_r.bf16(P, FC * TOWN)
        ckv_sb = ckv_r.bf16(P, S)
        kshared = [r.bf16(P, S) for r in ksh_r]
        qshared = [r.bf16(P, TOWN) for r in qsh_r]

        kb1 = pr.dma("g", kshared[0][96:97, :], keybias[:])
        kb2 = pr.dma("g", kshared[1][96:97, :], keybias[:])
        qo1 = pr.emit("v", lambda h: h.memset(qshared[0][96:97, :], 1.0), inc="v")
        qo2 = pr.emit("v", lambda h: h.memset(qshared[1][96:97, :], 1.0), inc="v")

        p3 = Region(A, 56 * 1024)
        x1b = p3.f32(P, FC * CH, byteoff=0)
        xn1 = p3.f32(P, FC * CH, byteoff=FC * CH * 4)
        o3 = 2 * FC * CH * 4
        abuf3 = p3.f32(P, CH, byteoff=o3)
        bbuf3 = p3.f32(P, CH, byteoff=o3 + 2048)
        csb1 = p3.f32(96, CH, byteoff=o3 + 4096)
        csb2 = p3.f32(96, CH, byteoff=o3 + 6144)
        rop1 = p3.f32(96, CH, byteoff=o3 + 8192)
        m8b = p3.f32(P, CH, byteoff=o3 + 10240)
        sxb = p3.f32(P, CH, byteoff=o3 + 12288)
        tq3 = p3.f32(P, CH, byteoff=o3 + 14336)
        wdb3 = p3.f32(P, D, byteoff=o3 + 16384)
        wdacc = wdacc_r.f32(P, NHC1)

        x1use = []
        ab3_use = []
        cs_use = []
        rot_use = []
        wd3_use = []
        wd_sum_last = None
        for cc in range(NCH):
            own = cc >= 4
            lt = []
            for fc in range(FC):
                lt.append(pr.dma("y", x1b[:, fc * CH:(fc + 1) * CH],
                                 x1_scr[fc, :, cc * CH:(cc + 1) * CH],
                                 waits=x1use + [x1st_all[-1]]))
            x1use = []
            ra = tm_to_row_pe(a1_tm[:, 4 * cc:4 * cc + 4], 4, 0, 5, waits=[])
            rb = tm_to_row_pe(b1_tm[:, 4 * cc:4 * cc + 4], 4, 1, 5, waits=[])
            ba = bcast(5, 0, CH, waits=[ra])
            ca = pr.emit("v", lambda h: h.tensor_copy(abuf3, PS[5][0:P, 0:CH]),
                         waits=[ba] + ab3_use, inc="v")
            ab3_use = []
            bg[5].append(ca)
            bb_ = bcast(5, 1, CH, waits=[rb])
            cb = pr.emit("v", lambda h: h.tensor_copy(bbuf3, PS[5][0:P, 0:CH]),
                         waits=[bb_], inc="v")
            bg[5].append(cb)
            lnt = []
            for fc in range(FC):
                xsl = x1b[:, fc * CH:(fc + 1) * CH]
                xnsl = xn1[:, fc * CH:(fc + 1) * CH]
                t1 = TT("v", xnsl, xsl, abuf3, ALU.mult, waits=[ca] + lt)
                t2 = TT("v", xnsl, xnsl, bbuf3, ALU.add, waits=[cb, t1])
                dst = (xnb_own[:, fc * TOWN + (cc - 4) * CH:
                               fc * TOWN + (cc - 3) * CH] if own
                       else xn1[0:0, 0:0])
                if own:
                    t3 = pr.emit("v", lambda h, dst=dst, xnsl=xnsl: h.tensor_copy(
                        dst, xnsl), waits=[t2], inc="v")
                    lnt.append(t3)
                else:
                    lnt.append(t2)
            ab3_use = [lnt[-1]]

            def xnb_src(fc, cc=cc, own=own):
                return None

            xnbc = []
            for fc in range(FC):
                if own:
                    src = xnb_own[:, fc * TOWN + (cc - 4) * CH:
                                  fc * TOWN + (cc - 3) * CH]
                    xnbc.append((src, lnt[fc]))
                else:
                    dst = xq_own[:, fc * TOWN: fc * TOWN + CH]
                    tk = pr.emit("v", lambda h, dst=dst, fc=fc: h.tensor_copy(
                        dst, xn1[:, fc * CH:(fc + 1) * CH]), waits=[lnt[fc]],
                        inc="v")
                    xnbc.append((dst, tk))

            mmc = None
            for fc in range(FC):
                src, tk = xnbc[fc]
                mmc = MM(PS[0][0:P, 0:CH], wkv1[:, fc * DC:(fc + 1) * DC], src,
                         start=(fc == 0), stop=(fc == FC - 1),
                         waits=[tk] + (bg[0] if fc == 0 else []))
                if fc == 0:
                    bg[0] = []
            ckv_cp = pr.emit("v", lambda h, cc=cc: h.tensor_copy(
                ckv_sb[:, cc * CH:(cc + 1) * CH], PS[0][0:P, 0:CH]),
                waits=[mmc], inc="v")
            bg[0] = [ckv_cp]
            c1t = pr.dma("y", csb1[64:96, :], cs1k[:, cc * CH:(cc + 1) * CH],
                         waits=cs_use)
            c2t = pr.dma("y", csb2[64:96, :], cs2k[:, cc * CH:(cc + 1) * CH])
            cs_use = []
            mk1 = MM(PS[6][64:96, 0:CH], wkr1[:], ckv_sb[:, cc * CH:(cc + 1) * CH],
                     waits=[ckv_cp] + bg[6], tile_position=(0, 64))
            bg[6] = []
            mk2 = MM(PS[2][64:96, 0:CH], wkr2[:], ckv_sb[:, cc * CH:(cc + 1) * CH],
                     waits=[ckv_cp] + bg[2], tile_position=(0, 64))
            bg[2] = []
            r1 = TT("v", rop1[64:96, :], PS[6][64:96, 0:CH], csb1[64:96, :],
                    ALU.mult, waits=[mk1, c1t] + rot_use)
            rot_use = []
            bg[6] = [r1]
            tkr = TT("v", csb1[64:96, :], PS[2][64:96, 0:CH], csb2[64:96, :],
                     ALU.mult, waits=[mk2, c2t, r1])
            bg[2] = [tkr]
            rot = TT("v", kshared[0][64:96, cc * CH:(cc + 1) * CH],
                     rop1[64:96, :], csb1[64:96, :], ALU.add, waits=[tkr])
            rot2 = pr.emit("v", lambda h, cc=cc: h.tensor_copy(
                kshared[1][64:96, cc * CH:(cc + 1) * CH],
                kshared[0][64:96, cc * CH:(cc + 1) * CH]), waits=[rot], inc="v")
            cs_use = [rot]
            rot_use = [rot]
            last_tok = rot2
            if own:
                oc4 = cc - 4
                c3t = pr.dma("y", csb1[64:96, :], cs1q[:, oc4 * CH:(oc4 + 1) * CH],
                             waits=[rot])
                c4t = pr.dma("y", csb2[64:96, :], cs2q[:, oc4 * CH:(oc4 + 1) * CH],
                             waits=[tkr])
                cs_use = []
                mq1 = mq2 = None
                for fc in range(FC):
                    src, tk = xnbc[fc]
                    mq1 = MM(PS[6][64:96, 0:CH], wqr1[:, fc * DR:(fc + 1) * DR],
                             src, start=(fc == 0), stop=(fc == FC - 1),
                             waits=[tk] + (bg[6] if fc == 0 else []),
                             tile_position=(0, 64))
                    if fc == 0:
                        bg[6] = []
                    mq2 = MM(PS[2][64:96, 0:CH], wqr2[:, fc * DR:(fc + 1) * DR],
                             src, start=(fc == 0), stop=(fc == FC - 1),
                             waits=(bg[2] if fc == 0 else []),
                             tile_position=(0, 64))
                    if fc == 0:
                        bg[2] = []
                r1q = TT("v", rop1[64:96, :], PS[6][64:96, 0:CH], csb1[64:96, :],
                         ALU.mult, waits=[mq1, c3t])
                bg[6] = [r1q]
                t2q = TT("v", csb1[64:96, :], PS[2][64:96, 0:CH], csb2[64:96, :],
                         ALU.mult, waits=[mq2, c4t, r1q])
                bg[2] = [t2q]
                rq = TT("v", qshared[0][64:96, oc4 * CH:(oc4 + 1) * CH],
                        rop1[64:96, :], csb1[64:96, :], ALU.add, waits=[t2q])
                rq2 = pr.emit("v", lambda h, oc4=oc4: h.tensor_copy(
                    qshared[1][64:96, oc4 * CH:(oc4 + 1) * CH],
                    qshared[0][64:96, oc4 * CH:(oc4 + 1) * CH]),
                    waits=[rq], inc="v")
                cs_use = [rq]
                rot_use = [rq]
                mm8 = pr.emit("v", lambda h: h.reduce_max(
                    m8b, xn1.rearrange("p (c n) -> p n c", c=FC), axis=AX.X,
                    apply_absolute_value=True), waits=lnt, inc="v")
                trts = []
                for q4 in range(4):
                    trt = pr.emit("t", lambda h, q4=q4: h.transpose(
                        PS[3][0:P, 0:P], m8b[:, q4 * P:(q4 + 1) * P], ident32[:]),
                        waits=[mm8] + bg[3], inc="t")
                    bg[3] = []
                    rmx = pr.emit("v", lambda h, q4=q4: h.reduce_max(
                        tmsc[:, 20 + q4:21 + q4], PS[3][0:P, 0:P], axis=AX.X),
                        waits=[trt], inc="v")
                    bg[3] = [rmx]
                    trts.append(rmx)
                aq1 = pr.emit("v", lambda h: h.tensor_scalar_max(
                    tmsc[:, 20:24], tmsc[:, 20:24], 1e-5), waits=trts, inc="v")
                aq2 = pr.emit("v", lambda h, oc4=oc4: h.tensor_scalar_mul(
                    isx_tm[:, oc4 * 4:(oc4 + 1) * 4], tmsc[:, 20:24],
                    1.0 / 127.0), waits=[aq1], inc="v")
                aq3 = pr.emit("v", lambda h: h.reciprocal(
                    tmsc[:, 24:28], tmsc[:, 20:24]), waits=[aq2], inc="v")
                aq4 = pr.emit("v", lambda h: h.tensor_scalar_mul(
                    tmsc[:, 24:28], tmsc[:, 24:28], 127.0), waits=[aq3], inc="v")
                sxr = tm_to_row_pe(tmsc[:, 24:28], 4, 2, 4, waits=[aq4])
                bsx = bcast(4, 2, CH, waits=[sxr])
                csx = pr.emit("v", lambda h: h.tensor_copy(sxb, PS[4][0:P, 0:CH]),
                              waits=[bsx], inc="v")
                bg[4] = [csx]
                qts = []
                prev_qt = None
                for fc in range(FC):
                    xnsl = xn1[:, fc * CH:(fc + 1) * CH]
                    tqs = TT("v", tq3, xnsl, sxb, ALU.mult, waits=[csx, prev_qt])
                    qt = pr.emit("v", lambda h, fc=fc, oc4=oc4: h.tensor_scalar(
                        xq_own[:, fc * TOWN + oc4 * CH:
                               fc * TOWN + (oc4 + 1) * CH],
                        tq3, RC, RC, ALU.add, ALU.subtract), waits=[tqs], inc="v")
                    prev_qt = qt
                    qts.append(qt)
                last_tok = qts[-1]
            x1use = [last_tok]
            for k in range(4):
                hc = cc * 4 + k
                wdt = pr.dma("s", wdb3, l1_wd[hc], waits=wd3_use)
                rt = pr.emit("v", lambda h, hc=hc: h.reduce_sum(
                    wdacc[:, hc:hc + 1], wdb3, axis=AX.X,
                    apply_absolute_value=True), waits=[wdt], inc="v")
                wd3_use = [rt]
                wd_sum_last = rt

        wd_rs = pr.emit("v", lambda h: h.reduce_sum(
            tmsc[:, 18:19], wdacc[:], axis=AX.X), waits=[wd_sum_last], inc="v")
        mmd = MM(PS[5][0:1, 0:1], ones_col[:], tmsc[:, 18:19],
                 waits=[wd_rs] + bg[5])
        bg[5] = []
        cpd = pr.emit("v", lambda h: h.tensor_copy(
            rowbuf[0:1, 8:9], PS[5][0:1, 0:1]), waits=[mmd] + rowg[0], inc="v")
        rowg[0] = []
        bg[5] = [cpd]
        e1_ = pr.emit("v", lambda h: h.tensor_scalar(
            rowbuf[0:1, 8:9], rowbuf[0:1, 8:9], 1.0 / (H1 * D), 1e-5,
            ALU.mult, ALU.max), waits=[cpd], inc="v")
        e2_ = pr.emit("v", lambda h: h.reciprocal(
            rowbuf[0:1, 9:10], rowbuf[0:1, 8:9]), waits=[e1_], inc="v")
        bcd = bcast(5, 0, 2, waits=[e2_], rowap=rowbuf[0:1, 8:10])
        rowg[0].append(bcd)
        cpd2 = pr.emit("v", lambda h: h.tensor_copy(
            tmsc[:, 30:32], PS[5][0:P, 0:2]), waits=[bcd], inc="v")
        bg[5] = [cpd2]
        mde = pr.emit("v", lambda h: h.tensor_copy(mcol[:, 2:3], tmsc[:, 30:31]),
                      waits=[cpd2], inc="v")
        swd = pr.emit("v", lambda h: h.tensor_copy(mcol[:, 5:6], tmsc[:, 31:32]),
                      waits=[cpd2], inc="v")

        if debug:
            pr.barrier()
            pr.dma("g", dbg["d_ckv"], ckv_sb[:])
            pr.dma("y", dbg["d_mw"], mcol[:])
            pr.dma("y", dbg["d_isx"], isx_tm[:])
            for fc in range(FC):
                pr.dma("g", dbg["d_xq"][fc], xq_own[:, fc * TOWN:(fc + 1) * TOWN])
            pr.dma("y", dbg["d_ab"][0], a0_tm[:])
            pr.dma("y", dbg["d_ab"][1], b0_tm[:])
            pr.dma("y", dbg["d_ab"][2], a1_tm[:])
            pr.dma("y", dbg["d_ab"][3], b1_tm[:])
            pr.barrier()

        pr.barrier()
        p3.free()
        wdacc_r.free()

        vt_r = Region(A, NSLOT * 130 * 2)
        attnr_r = Region(A, 1024 * 4 + 256)
        attnfh_r = Region(A, 2 * TOWN * 2)
        pbuf_r = Region(A, 2 * 1024 * 2)
        rb3_r = Region(A, CH * 4)
        wqh_r = Region(A, 2 * CH * 2)
        tern_r = Region(A, 2 * (D * 4 + D * 2))
        dm_r = Region(A, 4 * CH * 2)
        dmask_sb = dm_r.bf16(P, 4 * CH)
        dmt = pr.dma("g", dmask_sb.rearrange("p (r n) -> p r n", r=4),
                     dmask.transpose([1, 0, 2]))

        vtaug = vt_r.bf16(P, NSLOT * 130)
        attnraw = attnr_r.f32(65, 1024)
        rb3 = rb3_r.f32(64, CH)

        tern_jobs = []
        for m, src, dst, swcol in (("g", l1_wg, tern_g, 3), ("u", l1_wu, tern_u, 4),
                                   ("d", l1_wd, tern_d, 5)):
            for hc in range(NHC1):
                tern_jobs.append((src, dst, swcol, hc))
        tern_use = [[], []]
        tern_idx = [0]
        tern_stores = []

        def emit_tern(njobs):
            for _ in range(njobs):
                if tern_idx[0] >= len(tern_jobs):
                    return
                src, dst, swcol, hc = tern_jobs[tern_idx[0]]
                i = tern_idx[0] % 2
                tern_idx[0] += 1
                win = tern_r.f32(P, D, byteoff=i * (D * 4 + D * 2))
                wout = tern_r.bf16(P, D, byteoff=i * (D * 4 + D * 2) + D * 4)
                wtk = pr.dma("s", win, src[hc], waits=tern_use[i])
                tern_use[i] = []
                sw = mcol[:, swcol:swcol + 1]
                t1 = pr.emit("v", lambda h, win=win, sw=sw: h.tensor_scalar(
                    win, win, sw, RC, ALU.mult, ALU.add),
                    waits=[wtk, cpm2, swd], inc="v")
                t2 = pr.emit("v", lambda h, win=win: h.tensor_scalar(
                    win, win, RC, -1.0, ALU.subtract, ALU.max), waits=[t1],
                    inc="v")
                t3 = pr.emit("v", lambda h, win=win, wout=wout: h.tensor_scalar_min(
                    wout, win, 1.0), waits=[t2], inc="v")
                stk = pr.dma("g", dst[hc], wout, waits=[t3])
                tern_use[i] = [stk]
                tern_stores.append(stk)

        def kt_count(qc):
            return 16 + 4 * qc + 4

        kshared_use = [[kb1], [kb2]]
        qshared_use = [[qo1], [qo2]]
        wqh_use = [[], []]
        vt_use = []
        pbuf_use = [[], []]
        attnf_toks = []
        attnfh_use = [[], []]
        attnf_stores = []
        for hg in range(NH // 2):
            mzv = pr.emit("v", lambda h: h.memset(vtaug[:], 1.0),
                          waits=vt_use, inc="v")
            vt_use = []
            vt_toks = []
            for tt in range(NSLOT):
                mmv = MM(PS[6][0:P, 0:P], ckv_sb[:, tt * P:(tt + 1) * P],
                         wvup1[:, hg * 128:(hg + 1) * 128],
                         waits=bg[6] + [mzv])
                bg[6] = []
                out_ap = (vtaug[:, tt * 130:(tt + 1) * 130]
                          .rearrange("p (g n) -> p g n", g=2)[:, :, 0:64])
                ev = pr.emit("v", lambda h, out_ap=out_ap: h.tensor_copy(
                    out_ap, PS[6][0:P, 0:P].rearrange("p (g n) -> p g n", g=2)),
                    waits=[mmv], inc="v")
                bg[6] = [ev]
                vt_toks.append(ev)
            vt_done = vt_toks[-1]
            for hh in range(2):
                h_ = 2 * hg + hh
                hb = h_ % 2
                kev = []
                for cc in range(NCH):
                    mmk = MM(PS[0][0:64, 0:CH], wkup1[:, h_ * HD:(h_ + 1) * HD],
                             ckv_sb[:, cc * CH:(cc + 1) * CH],
                             waits=kshared_use[hb] + bg[0])
                    bg[0] = []
                    ev = pr.emit("v", lambda h, cc=cc, hb=hb: h.tensor_copy(
                        kshared[hb][0:64, cc * CH:(cc + 1) * CH],
                        PS[0][0:64, 0:CH]), waits=[mmk], inc="v")
                    bg[0] = [ev]
                    kev.append(ev)
                kshared_use[hb] = []
                wqh = wqh_r.bf16(P, CH, byteoff=hb * CH * 2)
                wqt = pr.dma("g", wqh.rearrange("p (c k) -> p c k", c=FC),
                             l1_wq[:, :, h_ * HD:(h_ + 1) * HD].transpose([1, 0, 2]),
                             waits=wqh_use[hb])
                wqh_use[hb] = []
                wqs = pr.emit("v", lambda h, wqh=wqh: h.tensor_scalar_mul(
                    wqh, wqh, INV_SQRT96), waits=[wqt], inc="v")
                qev = []
                mmq = None
                for tc in range(4):
                    for fc in range(FC):
                        mmq = MM(PS[0][0:64, 0:CH], wqh[:, fc * HD:(fc + 1) * HD],
                                 xnb_own[:, fc * TOWN + tc * CH:
                                         fc * TOWN + (tc + 1) * CH],
                                 start=(fc == 0), stop=(fc == FC - 1),
                                 waits=[wqs] + qshared_use[hb] +
                                       (bg[0] if fc == 0 else []))
                        if fc == 0:
                            bg[0] = []
                    ev = pr.emit("v", lambda h, tc=tc, hb=hb: h.tensor_copy(
                        qshared[hb][0:64, tc * CH:(tc + 1) * CH],
                        PS[0][0:64, 0:CH]), waits=[mmq], inc="v")
                    bg[0] = [ev]
                    qev.append(ev)
                qshared_use[hb] = []
                wqh_use[hb] = [mmq]
                emit_tern(3)
                kq_ready = [kev[-1], qev[-1]]
                s_mms = []
                for qp in range(2):
                    qc0, qc1 = 2 * qp, 2 * qp + 1
                    pvb0, pvb1 = 1, 6
                    pv_last0 = pv_last1 = None
                    pend = None
                    for kt in range(kt_count(qc1) + 1):
                        cur = None
                        if kt < kt_count(qc1):
                            in_qc0 = kt < kt_count(qc0)
                            sb0 = 2 + 2 * (kt % 2)
                            sb1_ = sb0 + 1
                            lhs = kshared[hb][0:97, kt * P:(kt + 1) * P]
                            pb = pbuf_r.bf16(P, 1024, byteoff=(kt % 2) * 1024 * 2)
                            wide = PSW[kt % 2]
                            smm0 = None
                            if in_qc0:
                                smm0 = MM(PS[sb0][0:P, 0:CH], lhs,
                                          qshared[hb][0:97,
                                                      qc0 * CH:(qc0 + 1) * CH],
                                          waits=kq_ready + bg[sb0] + bg[sb1_])
                                bg[sb0] = []
                            smm1 = MM(PS[sb1_][0:P, 0:CH], lhs,
                                      qshared[hb][0:97, qc1 * CH:(qc1 + 1) * CH],
                                      waits=kq_ready + bg[sb1_] + bg[sb0])
                            bg[sb0] = []
                            bg[sb1_] = []
                            s_mms.append(smm1)
                            if in_qc0:
                                ex1 = pr.emit("s", lambda h, pb=pb, wide=wide:
                                              h.activation(pb[:, 0:1024],
                                                           wide[0:P, 0:1024],
                                                           AF.Exp),
                                              waits=[smm0, smm1] +
                                                    pbuf_use[kt % 2], inc="s")
                                ex0 = ex1
                            else:
                                ex1 = pr.emit("s", lambda h, pb=pb, sb1_=sb1_:
                                              h.activation(pb[:, CH:1024],
                                                           PS[sb1_][0:P, 0:CH],
                                                           AF.Exp),
                                              waits=[smm1] + pbuf_use[kt % 2],
                                              inc="s")
                                ex0 = None
                            bg[sb0] = [ex1]
                            bg[sb1_] = [ex1]
                            pbuf_use[kt % 2] = []
                            mts = []
                            if kt >= 16 and (kt - 16) // 4 == qc0 and in_qc0:
                                r = (kt - 16) % 4
                                mt = pr.emit(
                                    "g", lambda h, pb=pb, r=r: h.tensor_tensor(
                                        pb[:, 0:CH], pb[:, 0:CH],
                                        dmask_sb[:, r * CH:(r + 1) * CH],
                                        ALU.mult),
                                    waits=[ex0, dmt], inc="g")
                                mts.append(mt)
                            if kt >= 16 and (kt - 16) // 4 == qc1:
                                r = (kt - 16) % 4
                                mt = pr.emit(
                                    "g", lambda h, pb=pb, r=r: h.tensor_tensor(
                                        pb[:, CH:1024], pb[:, CH:1024],
                                        dmask_sb[:, r * CH:(r + 1) * CH],
                                        ALU.mult),
                                    waits=[ex1, dmt], inc="g")
                                mts.append(mt)
                            cur = (kt, in_qc0, ex0, ex1, mts, pb)
                        if pend is not None:
                            pkt, p_in0, pex0, pex1, pmts, ppb = pend
                            vt_l = vtaug[:, pkt * 130 + hh * 65:
                                         pkt * 130 + hh * 65 + 65]
                            if p_in0:
                                pv_last0 = MM(PS[pvb0][0:65, 0:CH], vt_l,
                                              ppb[:, 0:CH],
                                              start=(pkt == 0),
                                              stop=(pkt == kt_count(qc0) - 1),
                                              waits=[pex0, vt_done] + pmts +
                                                    (bg[pvb0] if pkt == 0
                                                     else []))
                                if pkt == 0:
                                    bg[pvb0] = []
                                pbuf_use[pkt % 2].append(pv_last0)
                            pv_last1 = MM(PS[pvb1][0:65, 0:CH], vt_l,
                                          ppb[:, CH:1024],
                                          start=(pkt == 0),
                                          stop=(pkt == kt_count(qc1) - 1),
                                          waits=[pex1, vt_done] + pmts +
                                                (bg[pvb1] if pkt == 0 else []))
                            if pkt == 0:
                                bg[pvb1] = []
                            pbuf_use[pkt % 2].append(pv_last1)
                        pend = cur
                    ev0 = pr.emit("v", lambda h: h.tensor_copy(
                        attnraw[:, 0:CH], PS[pvb0][0:65, 0:CH]),
                        waits=[pv_last0] + attnf_toks[-1:], inc="v")
                    bg[pvb0] = [ev0]
                    ev1 = pr.emit("v", lambda h: h.tensor_copy(
                        attnraw[:, CH:1024], PS[pvb1][0:65, 0:CH]),
                        waits=[pv_last1], inc="v")
                    bg[pvb1] = [ev1]
                    d3 = pr.emit("v", lambda h: h.reciprocal(
                        row(2, 1024), attnraw[64:65, 0:1024]),
                        waits=[ev0, ev1] + rowg[2] + rowg[3], inc="v")
                    rowg[2] = []
                    rowg[3] = []
                    for (qq, half) in ((qc0, 0), (qc1, 1)):
                        bb3 = bcast(2, 2, CH, waits=[d3],
                                    rowap=row(2, 1024)[0:1, half * CH:
                                                       (half + 1) * CH])
                        rowg[2].append(bb3)
                        rowg[3].append(bb3)
                        cb3 = pr.emit("v", lambda h: h.tensor_copy(
                            rb3, PS[2][0:64, 0:CH]),
                            waits=[bb3] + attnf_toks[-1:], inc="v")
                        bg[2] = [cb3]
                        nt = TT("v", attnfh_r.bf16(64, TOWN, byteoff=hb * TOWN * 2)
                                [:, qq * CH:(qq + 1) * CH],
                                attnraw[0:64, half * CH:(half + 1) * CH], rb3,
                                ALU.mult, waits=[cb3] + (attnfh_use[hb]
                                                         if qp == 0 and half == 0
                                                         else []))
                        if qp == 0 and half == 0:
                            attnfh_use[hb] = []
                        attnf_toks.append(nt)
                afst = pr.dma("y", attnf_scr[h_],
                              attnfh_r.bf16(64, TOWN, byteoff=hb * TOWN * 2),
                              waits=attnf_toks[-4:])
                attnf_stores.append(afst)
                attnfh_use[hb] = [afst]
                kshared_use[hb] = s_mms[-1:]
                qshared_use[hb] = s_mms[-1:]
                vt_use = [pv_last1]
                emit_tern(3)

        pr.barrier()
        vt_r.free()
        attnr_r.free()
        attnfh_r.free()
        dm_r.free()
        pbuf_r.free()
        wqh_r.free()
        xnb_own_r.free()
        ckv_r.free()
        tern_r.free()
        rb3_r.free()
        for r_ in ksh_r:
            r_.free()
        for r_ in qsh_r:
            r_.free()
        attnf_r = [Region(A, NH // 2 * TOWN * 2), Region(A, NH // 2 * TOWN * 2)]

        def attnf_ap(h_, c0, c1):
            reg = attnf_r[h_ // 8]
            base = (h_ % 8) * TOWN
            return reg.bf16(64, NH // 2 * TOWN)[:, base + c0:base + c1]

        aflt = []
        for h_ in range(NH):
            aflt.append(pr.dma("y", attnf_ap(h_, 0, TOWN), attnf_scr[h_]))
        aflt_all = aflt[-8:]
        wo1_r = Region(A, NH * D * 2)
        s1_r = [Region(A, FC // 2 * TOWN * 2) for _ in range(2)]

        def s1_ap(oc, c0, c1):
            reg = s1_r[oc // 4]
            base = (oc % 4) * TOWN
            return reg.bf16(P, FC // 2 * TOWN)[:, base + c0:base + c1]

        wo1 = wo1_r.bf16(HD, NH * D)
        wot = pr.dma("g", wo1, l1_wo[:])
        s1_toks = []
        for oc in range(FC):
            for tc in range(4):
                bank = 0 if oc % 2 == 0 else 2
                mo = None
                for h_ in range(NH):
                    mo = MM(PS[bank][0:P, 0:CH],
                            wo1[0:HD, h_ * D + oc * P: h_ * D + (oc + 1) * P],
                            attnf_ap(h_, tc * CH, (tc + 1) * CH),
                            start=(h_ == 0), stop=(h_ == NH - 1),
                            waits=[wot] + aflt_all + (bg[bank] if h_ == 0 else []))
                    if h_ == 0:
                        bg[bank] = []
                ev = pr.emit("v", lambda h, oc=oc, tc=tc, bank=bank: h.tensor_copy(
                    s1_ap(oc, tc * CH, (tc + 1) * CH),
                    PS[bank][0:P, 0:CH]), waits=[mo], inc="v")
                bg[bank] = [ev]
                s1_toks.append(ev)
        if debug:
            for fc in range(FC):
                pr.dma("g", dbg["d_s1"][fc], s1_ap(fc, 0, TOWN),
                       waits=s1_toks[-1:])

        pr.barrier()
        attnf_r[0].free()
        attnf_r[1].free()
        wo1_r.free()

        gu_r = Region(A, NHC1 * CH * 4)
        x1o_r = Region(A, FC * CH * 4)
        tw_r = Region(A, 2 * 2 * D * 2)
        p4 = Region(A, 8 * 1024)
        p4b = Region(A, 8 * 1024)
        isxb = p4.f32(P, CH, byteoff=0)
        sgub = p4.f32(P, CH, byteoff=2048)
        adb = p4.f32(P, CH, byteoff=4096)
        tg4 = p4.f32(P, CH, byteoff=6144)
        sg4 = p4b.f32(P, CH, byteoff=0)
        gg4 = p4b.f32(P, CH, byteoff=2048)
        s1f4 = p4b.f32(P, CH, byteoff=4096)
        m84 = p4b.f32(P, CH, byteoff=6144)

        p4c = Region(A, 4 * 1024)
        mxb = p4c.f32(P, CH, byteoff=0)
        mnb = p4c.f32(P, CH, byteoff=2048)
        gu = gu_r.f32(P, NHC1 * CH)
        xqd = gu_r.bf16(P, NHC1 * CH)
        x1o = x1o_r.f32(P, FC * CH)

        tw_use = [[], []]
        x1o_use = []
        isxb_use = []
        ystores = []
        for tc in range(4):
            cc = tc + 4
            rs = tm_to_row_pe(isx_tm[:, tc * 4:(tc + 1) * 4], 4, 0, 5, waits=[])
            bi = bcast(5, 0, CH, waits=[rs])
            ci = pr.emit("v", lambda h: h.tensor_copy(isxb, PS[5][0:P, 0:CH]),
                         waits=[bi] + isxb_use, inc="v")
            bg[5] = [ci]
            lt = []
            for fc in range(FC):
                lt.append(pr.dma("y", x1o[:, fc * CH:(fc + 1) * CH],
                                 x1_scr[fc, :, cc * CH:(cc + 1) * CH],
                                 waits=x1o_use))
            x1o_use = []
            gu_toks = []
            tg4_use = []
            mm_prev = trts[-1:] if tc > 0 else []
            mz_x = pr.emit("v", lambda h: h.memset(mxb, 0.0), waits=mm_prev,
                           inc="v")
            mz_n = pr.emit("v", lambda h: h.memset(mnb, 0.0), inc="v")
            mm_carry = [mz_x, mz_n]
            for hc in range(NHC1):
                i = hc % 2
                twg = tw_r.bf16(P, D, byteoff=i * 2 * D * 2)
                twu = tw_r.bf16(P, D, byteoff=i * 2 * D * 2 + D * 2)
                tg_t = pr.dma("y", twg, tern_g[hc],
                              waits=tw_use[i] + tern_stores[-8:])
                tu_t = pr.dma("y", twu, tern_u[hc], waits=[])
                tw_use[i] = []
                bkg, bku = hc % 2, 2 + hc % 2
                mmg = mmu = None
                for fc in range(FC):
                    xqs = xq_own[:, fc * TOWN + tc * CH: fc * TOWN + (tc + 1) * CH]
                    mmg = MM(PS[bkg][0:P, 0:CH], twg[:, fc * P:(fc + 1) * P],
                             xqs, start=(fc == 0), stop=(fc == FC - 1),
                             waits=[tg_t] + (bg[bkg] if fc == 0 else []))
                    if fc == 0:
                        bg[bkg] = []
                    mmu = MM(PS[bku][0:P, 0:CH], twu[:, fc * P:(fc + 1) * P],
                             xqs, start=(fc == 0), stop=(fc == FC - 1),
                             waits=[tu_t] + (bg[bku] if fc == 0 else []))
                    if fc == 0:
                        bg[bku] = []
                t1 = TT("v", tg4, PS[bkg][0:P, 0:CH], isxb, ALU.mult,
                        waits=[mmg, ci] + tg4_use)
                bg[bkg] = [t1]
                sg_ = pr.emit("s", lambda h: h.activation(
                    sg4, tg4, AF.Silu, scale=mcol[:, 0:1]),
                    waits=[t1, cpm2], inc="s")
                gut = TT("v", gu[:, hc * CH:(hc + 1) * CH], sg4,
                         PS[bku][0:P, 0:CH], ALU.mult, waits=[sg_, mmu])
                bg[bku] = [gut]
                tg4_use = [sg_]
                gu_toks.append(gut)
                tw_use[i] = [mmg, mmu]
                mxt = TT("v", mxb, mxb, gu[:, hc * CH:(hc + 1) * CH], ALU.max,
                         waits=[gut] + mm_carry)
                mnt = TT("v", mnb, mnb, gu[:, hc * CH:(hc + 1) * CH], ALU.min,
                         waits=[gut])
                mm_carry = []
            mneg = pr.emit("v", lambda h: h.tensor_scalar_mul(
                mnb, mnb, -1.0), waits=[mnt], inc="v")
            mm8 = TT("v", m84, mxb, mnb, ALU.max, waits=[mxt, mneg])
            trts = []
            for q4 in range(4):
                trt = pr.emit("t", lambda h, q4=q4: h.transpose(
                    PS[6][0:P, 0:P], m84[:, q4 * P:(q4 + 1) * P], ident32[:]),
                    waits=[mm8] + bg[6], inc="t")
                bg[6] = []
                rmx = pr.emit("v", lambda h, q4=q4: h.reduce_max(
                    tmsc[:, 20 + q4:21 + q4], PS[6][0:P, 0:P], axis=AX.X),
                    waits=[trt], inc="v")
                bg[6] = [rmx]
                trts.append(rmx)
            a1_ = pr.emit("v", lambda h: h.tensor_scalar_max(
                tmsc[:, 20:24], tmsc[:, 20:24], 1e-5), waits=trts, inc="v")
            a2_ = pr.emit("v", lambda h: h.reciprocal(
                tmsc[:, 24:28], tmsc[:, 20:24]), waits=[a1_], inc="v")
            a3_ = pr.emit("v", lambda h: h.tensor_scalar_mul(
                tmsc[:, 24:28], tmsc[:, 24:28], 127.0), waits=[a2_], inc="v")
            a4_ = pr.emit("v", lambda h: h.tensor_scalar_mul(
                tmsc[:, 28:32], tmsc[:, 20:24], 1.0 / 127.0), waits=[a3_], inc="v")
            a5_ = pr.emit("v", lambda h: h.tensor_scalar_mul(
                tmsc[:, 28:32], tmsc[:, 28:32], mcol[:, 1:2]), waits=[a4_], inc="v")
            a6_ = pr.emit("v", lambda h: h.tensor_scalar_mul(
                tmsc[:, 28:32], tmsc[:, 28:32], mcol[:, 2:3]),
                waits=[a5_, mde], inc="v")
            r1_ = tm_to_row_pe(tmsc[:, 24:28], 4, 2, 5, waits=[a6_])
            r2_ = tm_to_row_pe(tmsc[:, 28:32], 4, 3, 5, waits=[a6_])
            rix = tm_to_row_pe(isx_tm[:, tc * 4:(tc + 1) * 4], 4, 1, 5, waits=[])
            adr = pr.emit("v", lambda h: h.tensor_tensor(
                row(3), row(3), row(1), ALU.mult), waits=[r2_, rix], inc="v")
            rowg[1].append(adr)
            rowg[3].append(adr)
            bs1 = bcast(5, 2, CH, waits=[r1_])
            cs1_ = pr.emit("v", lambda h: h.tensor_copy(sgub, PS[5][0:P, 0:CH]),
                           waits=[bs1], inc="v")
            bg[5] = [cs1_]
            bs2 = bcast(5, 3, CH, waits=[adr])
            cs2_ = pr.emit("v", lambda h: h.tensor_copy(adb, PS[5][0:P, 0:CH]),
                           waits=[bs2], inc="v")
            bg[5] = [cs2_]
            xqd_toks = []
            tq_use = []
            for hc in range(NHC1):
                t1 = TT("v", tg4, gu[:, hc * CH:(hc + 1) * CH], sgub, ALU.mult,
                        waits=[cs1_] + tq_use)
                t2 = pr.emit("v", lambda h, hc=hc: h.tensor_scalar(
                    xqd[:, hc * CH:(hc + 1) * CH], tg4, RC, RC,
                    ALU.add, ALU.subtract), waits=[t1], inc="v")
                tq_use = [t2]
                xqd_toks.append(t2)
            gg_prev = []
            for grp in range(2):
                dmm = [None] * 4
                for hc in range(NHC1):
                    i = hc % 2
                    twd = tw_r.bf16(P, CH, byteoff=i * 2 * D * 2)
                    td_t = pr.dma("y", twd,
                                  tern_d[hc, :, grp * CH:(grp + 1) * CH],
                                  waits=tw_use[i])
                    tw_use[i] = []
                    for j in range(4):
                        dmm[j] = MM(PS[j][0:P, 0:CH], twd[:, j * P:(j + 1) * P],
                                    xqd[:, hc * CH:(hc + 1) * CH],
                                    start=(hc == 0), stop=(hc == NHC1 - 1),
                                    waits=[td_t, xqd_toks[hc]] +
                                          (bg[j] if hc == 0 else []))
                        if hc == 0:
                            bg[j] = []
                    tw_use[i] = [dmm[3]]
                for j in range(4):
                    oc = grp * 4 + j
                    t1 = TT("v", tg4, PS[j][0:P, 0:CH], adb, ALU.mult,
                            waits=[dmm[j], cs2_] + tg4_use)
                    bg[j] = [t1]
                    tg4_use = []
                    g2 = pr.emit("s", lambda h: h.activation(gg4, tg4, AF.Silu),
                                 waits=[t1] + gg_prev, inc="s")
                    s1c = pr.emit("v", lambda h, oc=oc, tc=tc: h.tensor_copy(
                        s1f4, s1_ap(oc, tc * CH, (tc + 1) * CH)),
                        waits=[g2] + s1_toks[-1:], inc="v")
                    t2 = TT("v", gg4, gg4, s1f4, ALU.mult, waits=[s1c])
                    gg_prev = [t2]
                    x1sl = x1o[:, oc * CH:(oc + 1) * CH]
                    t3 = TT("v", x1sl, x1sl, gg4, ALU.add, waits=[t2] + lt)
                    yst = pr.dma("y", y_out[oc, :, tc * CH:(tc + 1) * CH],
                                 x1sl, waits=[t3])
                    ystores.append(yst)
                    tg4_use = [g2]
            x1o_use = ystores[-FC:]
            isxb_use = [t1]

        pr.emit("y", None, waits=ystores[-1:])
        pr.barrier()

        pr.replay(block)
    return nc


def _host_inputs(ins, core):
    b, h = core // 2, core % 2
    x = np.asarray(ins["L0_attn_x"][b], dtype=np.float32)
    own = slice(h * TOWN, (h + 1) * TOWN)
    oth = slice((1 - h) * TOWN, (2 - h) * TOWN)
    xo = np.concatenate([x[oth], x[own]], axis=0)
    d = {}
    d["x_in"] = np.ascontiguousarray(xo.T.reshape(FC, P, S))
    d["l0_wq"] = np.ascontiguousarray(
        np.asarray(ins["L0_attn_Wq"]).T.reshape(FC, P, HD))
    d["l0_wk"] = np.ascontiguousarray(
        np.asarray(ins["L0_attn_Wk"]).T.reshape(FC, P, HD))
    d["l0_wv"] = np.ascontiguousarray(
        np.asarray(ins["L0_attn_Wv"]).T.reshape(FC, P, HD))
    d["l0_wo"] = np.ascontiguousarray(np.asarray(ins["L0_attn_Wo"]).T)
    d["l0_wg"] = np.ascontiguousarray(
        np.asarray(ins["L0_ffn_w_gate"]).T.reshape(FC, P, H0))
    d["l0_wu"] = np.ascontiguousarray(
        np.asarray(ins["L0_ffn_w_up"]).T.reshape(FC, P, H0))
    d["l0_wd"] = np.ascontiguousarray(
        np.asarray(ins["L0_ffn_w_down"]).T.reshape(H0 // P, P, D))
    d["l1_wkv"] = np.ascontiguousarray(
        np.asarray(ins["L1_attn_W_kv_down"]).reshape(FC, P, DC))
    d["l1_wkup"] = np.ascontiguousarray(np.asarray(ins["L1_attn_W_k_up"]))
    d["l1_wvup"] = np.ascontiguousarray(np.asarray(ins["L1_attn_W_v_up"]))
    d["l1_wq"] = np.ascontiguousarray(
        np.asarray(ins["L1_attn_W_q"]).reshape(FC, P, D))
    wqr = np.asarray(ins["L1_attn_W_q_rope"])
    d["l1_wqr1"] = np.ascontiguousarray(
        np.concatenate([wqr[:, :16], wqr[:, :16]], 1).reshape(FC, P, DR))
    d["l1_wqr2"] = np.ascontiguousarray(
        np.concatenate([wqr[:, 16:], wqr[:, 16:]], 1).reshape(FC, P, DR))
    wkr = np.asarray(ins["L1_attn_W_k_rope"])
    d["l1_wkr1"] = np.ascontiguousarray(np.concatenate([wkr[:, :16], wkr[:, :16]], 1))
    d["l1_wkr2"] = np.ascontiguousarray(np.concatenate([wkr[:, 16:], wkr[:, 16:]], 1))
    wo = np.asarray(ins["L1_attn_W_o"])
    d["l1_wo"] = np.ascontiguousarray(
        wo.reshape(NH, HD, D).transpose(1, 0, 2).reshape(HD, NH * D))
    d["l1_wg"] = np.ascontiguousarray(
        np.asarray(ins["L1_ffn_w_gate"]).reshape(NHC1, P, FC, P)
        .transpose(0, 3, 2, 1).reshape(NHC1, P, D))
    d["l1_wu"] = np.ascontiguousarray(
        np.asarray(ins["L1_ffn_w_up"]).reshape(NHC1, P, FC, P)
        .transpose(0, 3, 2, 1).reshape(NHC1, P, D))
    d["l1_wd"] = np.ascontiguousarray(
        np.asarray(ins["L1_ffn_w_down"]).reshape(FC, P, NHC1, P)
        .transpose(2, 3, 0, 1).reshape(NHC1, P, D))

    km = np.ones((HD, NSLOT), np.float32)
    if h == 0:
        km[:, :16] = 0.0
    d["kfmask"] = km
    kb = np.full((1, S), -16.0, np.float32)
    if h == 0:
        kb[0, :TOWN] = -30000.0
    d["keybias"] = kb
    r = np.arange(P)
    d["ut_mask"] = (r[:, None] <= r[None, :]).astype(np.float32)
    dm = np.zeros((4, P, CH), np.float32)
    q = np.arange(CH)
    for rr in range(4):
        dm[rr] = (q[None, :] >= (rr * P + r[:, None])).astype(np.float32)
    d["dmask"] = dm
    inv = (1.0 / 10000.0 ** (np.arange(0, DR, 2) / DR)).astype(np.float64)
    pos_q = (h * TOWN + np.arange(TOWN)).astype(np.float64)
    pos_k = np.concatenate([(1 - h) * TOWN + np.arange(TOWN),
                            h * TOWN + np.arange(TOWN)]).astype(np.float64)

    def cstab(pos):
        fr = np.outer(pos, inv)
        c = np.cos(fr).astype(np.float32).T
        s = np.sin(fr).astype(np.float32).T
        return (np.ascontiguousarray(np.concatenate([c, s], 0)),
                np.ascontiguousarray(np.concatenate([-s, c], 0)))

    d["cs1q"], d["cs2q"] = cstab(pos_q)
    d["cs1k"], d["cs2k"] = cstab(pos_k)
    d["identf"] = np.eye(P, dtype=np.float32)
    return d


_NC_CACHE = {}


def kernel(**inputs):
    if "nc" not in _NC_CACHE:
        _NC_CACHE["nc"] = build_nc(debug=False)
    nc = _NC_CACHE["nc"]
    in_maps = [_host_inputs(inputs, c) for c in range(8)]
    res = run_bass_kernel_spmd(nc, in_maps, core_ids=list(range(8)))
    out = np.empty((4, S, D), np.float32)
    for c in range(8):
        b, h = c // 2, c % 2
        y = np.asarray(res.results[c]["y"])
        out[b, h * TOWN:(h + 1) * TOWN, :] = y.reshape(D, TOWN).T
    return out


# version 08
# speedup vs baseline: 1.1788x; 1.1788x over previous
import math
import numpy as np
import ml_dtypes
from contextlib import ExitStack

import concourse.bass as bass
import concourse.mybir as mybir
from concourse.bass_utils import run_bass_kernel_spmd

F32 = mybir.dt.float32
BF16 = mybir.dt.bfloat16
I32 = mybir.dt.int32
AF = mybir.ActivationFunctionType
ALU = mybir.AluOpType
AX = mybir.AxisListType

P = 128
D = 1024
FC = 8
S = 4096
TOWN = 2048
NSLOT = 32
NCH = 8
CH = 512
H0 = 3072
H1 = 4096
NHC1 = H1 // P
NH = 16
HD = 64
DR = 32
DC = 128
RC = 12582912.0
P1B_NCH = NCH
P1B_NSL = 4
P1B_TAIL = True
P1B_OPS = 9
P1B_SUB = 9
SAFE = False
SAFE2 = False
INV_SQRT96 = 1.0 / math.sqrt(96.0)


class Prog:
    ENGINES = ("t", "v", "s", "g", "y")
    NLANE = 8

    def __init__(self, nc, ctx):
        self.nc = nc
        self.ops = {e: [] for e in self.ENGINES}
        self.sem = {}
        for name in ("t", "v", "s", "g"):
            self.sem[name] = ctx.enter_context(nc.semaphore("sem_" + name))
        for q in ("y", "g_", "s_"):
            for l in range(self.NLANE):
                name = f"d{q}{l}"
                self.sem[name] = ctx.enter_context(nc.semaphore("sem_" + name))
        self.cnt = {name: 0 for name in self.sem}
        self.waited = {e: {name: 0 for name in self.sem} for e in self.ENGINES}
        self.dma_idx = {"y": 0, "g": 0, "s": 0}

    def emit(self, eng, fn, waits=(), inc=None, inc_amt=1):
        tok = None
        if inc is not None:
            self.cnt[inc] += inc_amt
            tok = (inc, self.cnt[inc])
        real = []
        for w in waits:
            if w is None:
                continue
            sname, val = w
            if val > self.waited[eng][sname]:
                real.append((sname, val))
                self.waited[eng][sname] = val
        self.ops[eng].append((fn, tuple(real), tok))
        return tok

    def dma(self, q, out, in_, waits=(), slow=False):
        qkey = {"y": "y", "g": "g_", "s": "s_"}[q]
        i = self.dma_idx[q]
        self.dma_idx[q] = i + 1
        lane = i % self.NLANE
        semname = f"d{qkey}{lane}"
        lane_prev = (semname, self.cnt[semname])
        if slow:
            fn = lambda h, out=out, in_=in_: h.dma_start(
                out=out, in_=in_, allow_slow_non_contiguous=True)
        else:
            fn = lambda h, out=out, in_=in_: h.dma_start(out=out, in_=in_)
        return self.emit(q, fn, waits=list(waits) + [lane_prev],
                         inc=semname, inc_amt=16)

    def barrier(self):
        toks = [(n, self.cnt[n]) for n in self.sem if self.cnt[n] > 0]
        for e in self.ENGINES:
            real = []
            for (sname, val) in toks:
                if val > self.waited[e][sname]:
                    real.append((sname, val))
                    self.waited[e][sname] = val
            if real:
                self.ops[e].append((None, tuple(real), None))

    def replay(self, block):
        def run(eng, h):
            prev = {name: 0 for name in self.sem}
            for fn, waits, tok in self.ops[eng]:
                for (sname, val) in waits:
                    h.wait_ge(self.sem[sname], val)
                if fn is not None:
                    inst = fn(h)
                    if tok is not None:
                        amt = tok[1] - prev[tok[0]] if tok[0].startswith("d") else 1
                        prev[tok[0]] = tok[1]
                        inst.then_inc(self.sem[tok[0]], 16 if tok[0].startswith("d") else 1)

        @block.tensor
        def _(h):
            run("t", h)

        @block.vector
        def _(h):
            run("v", h)

        @block.scalar
        def _(h):
            run("s", h)

        @block.gpsimd
        def _(h):
            run("g", h)

        @block.sync
        def _(h):
            run("y", h)


class Arena:
    def __init__(self, ap, nbytes):
        self.ap = ap
        self.nbytes = nbytes
        self.free = [(0, nbytes)]

    def _alloc(self, nbytes, align):
        nbytes = (nbytes + align - 1) // align * align
        for i, (off, sz) in enumerate(self.free):
            aoff = (off + align - 1) // align * align
            if aoff + nbytes <= off + sz:
                repl = []
                if aoff > off:
                    repl.append((off, aoff - off))
                if off + sz > aoff + nbytes:
                    repl.append((aoff + nbytes, off + sz - (aoff + nbytes)))
                self.free[i:i + 1] = repl
                return aoff, nbytes
        raise MemoryError(f"arena OOM: want {nbytes}, free={self.free}")

    def _free(self, off, nbytes):
        self.free.append((off, nbytes))
        self.free.sort()
        merged = []
        for o, s in self.free:
            if merged and merged[-1][0] + merged[-1][1] == o:
                merged[-1] = (merged[-1][0], merged[-1][1] + s)
            else:
                merged.append([o, s])
        self.free = [tuple(m) for m in merged]


class Region:
    def __init__(self, arena, nbytes, align=64):
        self.arena = arena
        self.off, self.nb = arena._alloc(nbytes, align)

    def f32(self, rows, cols, byteoff=0):
        o = self.off + byteoff
        return self.arena.ap.bitcast(F32)[0:rows, o // 4: o // 4 + cols]

    def bf16(self, rows, cols, byteoff=0):
        o = self.off + byteoff
        return self.arena.ap.bitcast(BF16)[0:rows, o // 2: o // 2 + cols]

    def bf16r(self, r0, rows, cols, byteoff=0):
        o = self.off + byteoff
        return self.arena.ap.bitcast(BF16)[r0:r0 + rows,
                                           o // 2: o // 2 + cols]

    def free(self):
        self.arena._free(self.off, self.nb)


def build_nc(debug=False):
    nc = bass.Bass("TRN2", target_bir_lowering=False, debug=False)

    def din(name, shape, dt=F32):
        return nc.dram_tensor(name, shape, dt, kind="ExternalInput").ap()

    x_in = din("x_in", [FC, P, S])
    l0_wq = din("l0_wq", [FC, P, HD])
    l0_wk = din("l0_wk", [FC, P, HD])
    l0_wv = din("l0_wv", [FC, P, HD])
    l0_wo = din("l0_wo", [HD, D])
    l0_wg = din("l0_wg", [FC, P, H0], I32)
    l0_wu = din("l0_wu", [FC, P, H0], I32)
    l0_wd = din("l0_wd", [H0 // P, P, D], I32)
    l1_wkv = din("l1_wkv", [FC, P, DC])
    l1_wkup = din("l1_wkup", [DC, D])
    l1_wvup = din("l1_wvup", [DC, D])
    l1_wq = din("l1_wq", [FC, P, D])
    l1_wqr1 = din("l1_wqr1", [FC, P, DR])
    l1_wqr2 = din("l1_wqr2", [FC, P, DR])
    l1_wkr1 = din("l1_wkr1", [DC, DR])
    l1_wkr2 = din("l1_wkr2", [DC, DR])
    l1_wo = din("l1_wo", [HD, NH * D])
    l1_wg = din("l1_wg", [NHC1, P, D], F32)
    l1_wu = din("l1_wu", [NHC1, P, D], F32)
    l1_wd = din("l1_wd", [NHC1, P, D], F32)
    kfmask = din("kfmask", [HD, NSLOT])
    keybias = din("keybias", [1, S])
    ut_mask = din("ut_mask", [P, P])
    dmask = din("dmask", [4, P, CH])
    cs1q = din("cs1q", [DR, TOWN])
    cs2q = din("cs2q", [DR, TOWN])
    cs1k = din("cs1k", [DR, S])
    cs2k = din("cs2k", [DR, S])
    identf = din("identf", [P, P])

    y_out = nc.dram_tensor("y", [FC, P, TOWN], F32, kind="ExternalOutput").ap()
    dbg = {}
    if debug:
        for name, shape in [("d_x1", [FC, P, S]), ("d_s0", [FC, P, S]),
                            ("d_ab", [4, P, NSLOT]), ("d_xq", [FC, P, TOWN]),
                            ("d_ckv", [P, S]), ("d_s1", [FC, P, TOWN]),
                            ("d_mw", [P, 8]), ("d_isx", [P, NSLOT // 2]),
                            ("d_num", [65, S]), ("d_rec", [1, S]),
                            ("d_qf", [64, P]), ("d_kf", [64, P]),
                            ("d_abf", [P, P]), ("d_vt", [P, 65])]:
            dbg[name] = nc.dram_tensor(name, shape, F32, kind="ExternalOutput").ap()

    s0_scr = nc.dram_tensor("s0_scr", [FC, P, S], BF16, kind="Internal").ap()
    x1_scr = nc.dram_tensor("x1_scr", [FC, P, S], F32, kind="Internal").ap()
    tern_g = nc.dram_tensor("tern_g", [NHC1, P, D], BF16, kind="Internal").ap()
    tern_u = nc.dram_tensor("tern_u", [NHC1, P, D], BF16, kind="Internal").ap()
    tern_d = nc.dram_tensor("tern_d", [NHC1, P, D], BF16, kind="Internal").ap()
    wd_bf_scr = nc.dram_tensor("wd_bf_scr", [H0 // P, P, D], BF16, kind="Internal").ap()
    attnf_scr = nc.dram_tensor("attnf_scr", [NH, HD, TOWN], BF16, kind="Internal").ap()
    drow = nc.dram_tensor("drow", [1, 16384], F32, kind="Internal").ap()

    def sbuf(name, shape, dt):
        return nc.alloc_sbuf_tensor(name, shape, dt).ap()

    wq0 = sbuf("wq0", [P, FC * HD], BF16)
    wk0 = sbuf("wk0", [P, FC * HD], BF16)
    wv0 = sbuf("wv0", [P, FC * HD], BF16)
    wo0 = sbuf("wo0", [HD, D], BF16)
    wkv1 = sbuf("wkv1", [P, FC * DC], BF16)
    wkup1 = sbuf("wkup1", [DC, D], BF16)
    wvup1 = sbuf("wvup1", [DC, D], BF16)
    wqr1 = sbuf("wqr1", [P, FC * DR], BF16)
    wqr2 = sbuf("wqr2", [P, FC * DR], BF16)
    wkr1 = sbuf("wkr1", [DC, DR], BF16)
    wkr2 = sbuf("wkr2", [DC, DR], BF16)
    kfm_sb = sbuf("kfm_sb", [HD, NSLOT], F32)
    utm_sb = sbuf("utm_sb", [P, P], F32)
    identb = sbuf("identb", [P, P], BF16)
    ident32 = sbuf("ident32", [P, P], F32)
    ones_col = sbuf("ones_col", [P, 1], F32)
    ones_row = sbuf("ones_row", [1, P], F32)
    a0_tm = sbuf("a0_tm", [P, NSLOT], F32)
    b0_tm = sbuf("b0_tm", [P, NSLOT], F32)
    a1_tm = sbuf("a1_tm", [P, NSLOT], F32)
    b1_tm = sbuf("b1_tm", [P, NSLOT], F32)
    isx_tm = sbuf("isx_tm", [P, NSLOT // 2], F32)
    mcol = sbuf("mcol", [P, 8], F32)
    rowbuf = sbuf("rowbuf", [1, 4096], F32)
    tmsc = sbuf("tmsc", [P, 64], F32)

    def row(i, n=CH):
        return rowbuf[0:1, i * CH:i * CH + n]

    AR_BYTES = (nc.sbuf_bytes_remaining - 3072) // 64 * 64
    arena_t = nc.alloc_sbuf_tensor("arena", [P, AR_BYTES], mybir.dt.int8).ap()
    A = Arena(arena_t, AR_BYTES)

    with ExitStack() as ctx:
        pr = Prog(nc, ctx)
        block = ctx.enter_context(nc.Block())
        PS = [nc.alloc_psum_tensor(f"ps{i}", [P, 512], F32).ap() for i in range(2)]
        PSW = [nc.alloc_psum_tensor(f"psw{i}", [P, 1024], F32).ap() for i in range(2)]
        PS.append(PSW[0][:, 0:512])
        PS.append(PSW[0][:, 512:1024])
        PS.append(PSW[1][:, 0:512])
        PS.append(PSW[1][:, 512:1024])
        PS.append(nc.alloc_psum_tensor("ps6", [P, 512], F32).ap())
        PSB = nc.alloc_psum_tensor("psb7", [P, 1024], BF16).ap()
        PS.append(None)

        waccr = Region(A, 2 * NHC1 * 4)
        wdacc_r = Region(A, NHC1 * 4)
        bg = [[] for _ in range(8)]
        rowg = [[] for _ in range(8)]

        def TT(eng, out, in0, in1, op, waits=()):
            return pr.emit(eng, lambda h: h.tensor_tensor(out, in0, in1, op),
                           waits=waits, inc=eng)

        def MM(out, lhsT, rhs, start=True, stop=True, waits=(), tile_position=None):
            def fn(h, out=out, lhsT=lhsT, rhs=rhs, start=start, stop=stop,
                   tp=tile_position):
                if tp is not None:
                    return h.matmul(out, lhsT, rhs, start=start, stop=stop,
                                    skip_group_check=True, tile_position=tp)
                return h.matmul(out, lhsT, rhs, start=start, stop=stop,
                                skip_group_check=True)
            return pr.emit("t", fn, waits=waits, inc="t")

        drow_off = [0]

        def _dslot(n):
            if drow_off[0] + n > 16384:
                drow_off[0] = 0
            off = drow_off[0]
            drow_off[0] += n
            return drow[0:1, off:off + n]

        def tm_to_row(tm_ap, w, slot, waits=()):
            d = _dslot(P * w)
            t1 = pr.dma("g", d.rearrange("a (j p) -> (a p) j", p=P), tm_ap,
                        waits=waits, slow=True)
            t2 = pr.dma("g", row(slot, P * w), d, waits=[t1] + rowg[slot])
            rowg[slot] = []
            return t2

        def row_to_tm(row_ap, w, dst_tm, waits=()):
            d = _dslot(P * w)
            t1 = pr.dma("g", d, row_ap, waits=waits)
            return pr.dma("g", dst_tm, d.rearrange("a (j p) -> (a p) j", p=P),
                          waits=[t1], slow=True)

        def bcast(bank, slot, n, waits=(), rowap=None):
            src = rowap if rowap is not None else row(slot, n)
            tok = MM(PS[bank][0:P, 0:n], ones_row[:], src,
                     waits=list(waits) + bg[bank])
            bg[bank] = []
            if rowap is None:
                rowg[slot].append(tok)
            return tok

        def tm_to_row_pe(tm_ap, w, slot, bank, waits=()):
            mm = None
            for k in range(w):
                mm = MM(PS[bank][0:1, k * P:(k + 1) * P], tm_ap[:, k:k + 1],
                        ident32[:], start=(k == 0), stop=(k == w - 1),
                        waits=(list(waits) + bg[bank]) if k == 0 else [])
                if k == 0:
                    bg[bank] = []
            cp = pr.emit("v", lambda h: h.tensor_copy(
                row(slot, P * w), PS[bank][0:1, 0:P * w]),
                waits=[mm] + rowg[slot], inc="v")
            rowg[slot] = []
            bg[bank] = [cp]
            return cp

        wt = []
        for dst, src in ((wq0, l0_wq), (wk0, l0_wk), (wv0, l0_wv),
                         (wkv1, l1_wkv), (wqr1, l1_wqr1), (wqr2, l1_wqr2)):
            wt.append(pr.dma("g", dst.rearrange("p (c k) -> p c k", c=FC),
                             src.transpose([1, 0, 2])))
        wt.append(pr.dma("g", wo0[:], l0_wo[:]))
        wt.append(pr.dma("g", wkup1[:], l1_wkup[:]))
        wt.append(pr.dma("g", wvup1[:], l1_wvup[:]))
        wt.append(pr.dma("g", wkr1[:], l1_wkr1[:]))
        wt.append(pr.dma("g", wkr2[:], l1_wkr2[:]))
        wt.append(pr.dma("y", kfm_sb[:], kfmask[:]))
        wt.append(pr.dma("y", utm_sb[:], ut_mask[:]))
        wt.append(pr.dma("g", identb[:], identf[:]))
        wt.append(pr.dma("y", ident32[:], identf[:]))
        m1 = pr.emit("v", lambda h: h.memset(ones_col[:], 1.0), inc="v")
        m2 = pr.emit("v", lambda h: h.memset(ones_row[:], 1.0), inc="v")
        consts = [m1, m2]
        for w in (wqr1, wqr2):
            consts.append(pr.emit(
                "v", lambda h, w=w: h.tensor_scalar_mul(w[:], w[:], INV_SQRT96),
                waits=wt, inc="v"))
        wt_all = wt + consts

        affine_prev = [None]

        def ln_affine_chunk(sb, sb2, mmt, mmt2, a_tm, b_tm, cc):
            c1 = pr.emit("v", lambda h, sb=sb: h.tensor_copy(
                row(0), PS[sb][0:1, 0:CH]), waits=[mmt, affine_prev[0]] + rowg[0],
                inc="v")
            rowg[0] = []
            c2 = pr.emit("v", lambda h, sb2=sb2: h.tensor_copy(
                row(1), PS[sb2][0:1, 0:CH]), waits=[mmt2] + rowg[1], inc="v")
            rowg[1] = []
            bg[sb].append(c1)
            bg[sb2].append(c2)
            s_tm = tmsc[:, 0:4]
            s2_tm = tmsc[:, 4:8]
            m_tm = tmsc[:, 8:12]
            v_tm = tmsc[:, 12:16]
            mm = None
            for k in range(8):
                srcr = row(0) if k < 4 else row(1)
                kk = k % 4
                mm = MM(PS[5][0:P, k:k + 1], srcr[0:1, kk * P:(kk + 1) * P],
                        ones_row[0:1, 0:1], start=(k == 0), stop=(k == 7),
                        waits=([c1, c2] + bg[5]) if k == 0 else [])
                if k == 0:
                    bg[5] = []
            t1 = pr.emit("v", lambda h: h.tensor_copy(
                tmsc[:, 0:8], PS[5][0:P, 0:8]), waits=[mm], inc="v")
            t2 = t1
            bg[5] = [t1]
            rowg[0].append(mm)
            rowg[1].append(mm)
            aa = a_tm[:, 4 * cc:4 * cc + 4]
            bb = b_tm[:, 4 * cc:4 * cc + 4]
            e1 = pr.emit("v", lambda h: h.tensor_scalar_mul(m_tm, s_tm, 1.0 / D),
                         waits=[t1], inc="v")
            e2 = pr.emit("v", lambda h: h.tensor_scalar_mul(s2_tm, s2_tm, 1.0 / D),
                         waits=[t2], inc="v")
            e3 = TT("v", v_tm, m_tm, m_tm, ALU.mult, waits=[e1])
            e4 = TT("v", v_tm, s2_tm, v_tm, ALU.subtract, waits=[e2, e3])
            e5 = pr.emit("v", lambda h: h.tensor_scalar_add(v_tm, v_tm, 1e-5),
                         waits=[e4], inc="v")
            e6 = pr.emit("s", lambda h: h.activation(v_tm, v_tm, AF.Sqrt),
                         waits=[e5], inc="s")
            e7 = pr.emit("v", lambda h: h.reciprocal(aa, v_tm), waits=[e6], inc="v")
            e8 = pr.emit("v", lambda h: h.scalar_tensor_tensor(
                bb, m_tm, -1.0, aa, ALU.mult, ALU.mult), waits=[e7], inc="v")
            affine_prev[0] = e8
            return e8

        sq_carry = [None]

        def ln_stats_chunk(buf, sq, a_tm, b_tm, cc, waits):
            mmt = mmt2 = None
            prev_sq = sq_carry
            for fc in range(FC):
                sqt = pr.emit("s", lambda h, fc=fc: h.activation(
                    sq, buf[:, fc * CH:(fc + 1) * CH], AF.Square),
                    waits=list(waits) + [prev_sq[0]], inc="s")
                mmt = MM(PS[6][0:1, 0:CH], ones_col[:], buf[:, fc * CH:(fc + 1) * CH],
                         start=(fc == 0), stop=(fc == FC - 1),
                         waits=list(waits) + (bg[6] if fc == 0 else []) + [consts[0]])
                if fc == 0:
                    bg[6] = []
                mmt2 = MM(PS[4][0:1, 0:CH], ones_col[:], sq,
                          start=(fc == 0), stop=(fc == FC - 1),
                          waits=[sqt] + (bg[4] if fc == 0 else []))
                if fc == 0:
                    bg[4] = []
                prev_sq[0] = mmt2
            return ln_affine_chunk(6, 4, mmt, mmt2, a_tm, b_tm, cc)

        xr = Region(A, 2 * FC * CH * 4)
        sqA_r = Region(A, CH * 4)
        sqA = sqA_r.f32(P, CH)
        xuse = [None, None]
        for cc in range(NCH):
            buf = xr.f32(P, FC * CH, byteoff=(cc % 2) * FC * CH * 4)
            lt = []
            for fc in range(FC):
                w = [xuse[cc % 2]] if xuse[cc % 2] else []
                lt.append(pr.dma("y", buf[:, fc * CH:(fc + 1) * CH],
                                 x_in[fc, :, cc * CH:(cc + 1) * CH], waits=w))
            xuse[cc % 2] = ln_stats_chunk(buf, sqA, a0_tm, b0_tm, cc, lt)


        wg_r = Region(A, FC * H0 * 2)
        wg_sb = wg_r.bf16(P, FC * H0)
        wgt_toks = []
        wbr = Region(A, 4 * D * 4)
        wacc = {m: waccr.f32(P, NHC1, byteoff=i * NHC1 * 4)
                for i, m in enumerate(("g", "u"))}
        wsum_done = {}
        prev_rt = [None, None, None, None]
        wdc_r = Region(A, 2 * D * 2)
        wdcvt_toks = []
        wdc_use = [[], []]
        p1w_jobs = [("G", fc) for fc in range(FC)]
        p1w_jobs += [("w", mi, m, src, hc)
                     for mi, (m, src) in enumerate((("g", l1_wg), ("u", l1_wu)))
                     for hc in range(NHC1)]
        p1w_jobs += [("c", hc) for hc in range(H0 // P)]
        p1w_pend = []
        p1w_pos = [0]

        def p1w_step(ndma):
            for fn in p1w_pend:
                fn()
            p1w_pend.clear()
            for _ in range(ndma):
                if p1w_pos[0] >= len(p1w_jobs):
                    return
                job = p1w_jobs[p1w_pos[0]]
                p1w_pos[0] += 1
                if job[0] == "G":
                    _, fc = job
                    wgt_toks.append(pr.dma(
                        "g", wg_sb[:, fc * H0:(fc + 1) * H0], l0_wg[fc]))
                elif job[0] == "w":
                    _, mi, m, src, hc = job
                    j = mi * NHC1 + hc
                    buf = wbr.f32(P, D, byteoff=(j % 4) * D * 4)
                    wtk = pr.dma("s", buf, src[hc, :, :],
                                 waits=[prev_rt[j % 4]] if prev_rt[j % 4] else [])

                    def red(buf=buf, m=m, hc=hc, j=j, wtk=wtk):
                        prev_rt[j % 4] = pr.emit(
                            "v", lambda h: h.reduce_sum(
                                wacc[m][:, hc:hc + 1], buf, axis=AX.X,
                                apply_absolute_value=True),
                            waits=[wtk], inc="v")
                        if hc == NHC1 - 1:
                            wsum_done[m] = prev_rt[j % 4]
                    p1w_pend.append(red)
                else:
                    _, hc = job
                    b_ = wdc_r.bf16(P, D, byteoff=(hc % 2) * D * 2)
                    t1_ = pr.dma("g", b_, l0_wd[hc], waits=wdc_use[hc % 2])
                    t2_ = pr.dma("g", wd_bf_scr[hc], b_, waits=[t1_])
                    wdc_use[hc % 2] = [t2_]
                    wdcvt_toks.append(t2_)

        def p1w_flush():
            while p1w_pos[0] < len(p1w_jobs) or p1w_pend:
                p1w_step(4)

        num_r = Region(A, 4 * S + 64)
        num_sb = num_r.f32(65, S)
        outn_r = Region(A, 2 * S)
        outn_sb = outn_r.bf16(64, S)
        sm = Region(A, 34 * 1024)
        tscr = sm.f32(P, CH, byteoff=0)
        o1 = CH * 4
        qrel = sm.f32(64, P, byteoff=o1)
        qmin = sm.f32(64, P, byteoff=o1 + 512)
        qexp = sm.f32(64, P, byteoff=o1 + 1024)
        krel = sm.f32(64, P, byteoff=o1 + 1536)
        kmin = sm.f32(64, P, byteoff=o1 + 2048)
        kexp = sm.f32(64, P, byteoff=o1 + 2560)
        ksum = sm.f32(64, P, byteoff=o1 + 3072)
        qf_bf = sm.bf16(64, P, byteoff=o1 + 3584)
        kf_bf = sm.bf16(64, P, byteoff=o1 + 3840)
        v_bf = sm.bf16(64, P, byteoff=o1 + 4352)
        kT_sb = sm.bf16(P, HD, byteoff=o1 + 4864)
        vT1_sb = sm.bf16(P, 96, byteoff=o1 + 5376)
        a_bf = sm.bf16(P, P, byteoff=o1 + 5888)
        state_bf = sm.bf16(64, 96, byteoff=o1 + 6400)
        state_f32 = sm.f32(64, 96, byteoff=o1 + 6656)
        abuf = sm.f32(P, CH, byteoff=o1 + 8192)
        bbuf = sm.f32(P, CH, byteoff=o1 + 8192 + 2048)
        xnbs = sm.bf16(P, FC * CH, byteoff=o1 + 8192 + 4096)
        rb_sb = sm.f32(64, CH, byteoff=o1 + 8192 + 4096 + FC * CH * 2)

        mz1 = pr.emit("v", lambda h: h.memset(state_bf, 0.0), inc="v")
        mz1b = pr.emit("v", lambda h: h.memset(state_f32, 0.0), inc="v")
        mz1c = pr.emit("v", lambda h: h.memset(vT1_sb[:, HD + 1:96], 0.0), inc="v")
        elu_prev = [None, None]
        mz2 = pr.emit("v", lambda h: h.memset(vT1_sb[:, HD:HD + 1], 1.0), inc="v")

        prev_snap = mz1
        numcp_last = None
        abuf_use = []
        bbuf_use = []
        xnbs_use = []
        elu_use = []
        tr_use = []
        a_use = []
        n_use = []
        st_use = []
        small_use = []
        for cc in range(P1B_NCH):
            buf = xr.f32(P, FC * CH, byteoff=(cc % 2) * FC * CH * 4)
            lt = []
            for fc in range(FC):
                w = [xuse[cc % 2]] if xuse[cc % 2] else []
                lt.append(pr.dma("y", buf[:, fc * CH:(fc + 1) * CH],
                                 x_in[fc, :, cc * CH:(cc + 1) * CH], waits=w))
            ra = tm_to_row_pe(a0_tm[:, 4 * cc:4 * cc + 4], 4, 0, 5,
                              waits=[affine_prev[0]])
            rb = tm_to_row_pe(b0_tm[:, 4 * cc:4 * cc + 4], 4, 1, 5,
                              waits=[affine_prev[0]])
            ba = bcast(5, 0, CH, waits=[ra])
            ca = pr.emit("v", lambda h: h.tensor_copy(abuf, PS[5][0:P, 0:CH]),
                         waits=[ba] + abuf_use, inc="v")
            abuf_use = []
            bg[5].append(ca)
            bb_ = bcast(5, 1, CH, waits=[rb])
            cb = pr.emit("v", lambda h: h.tensor_copy(bbuf, PS[5][0:P, 0:CH]),
                         waits=[bb_] + bbuf_use, inc="v")
            bbuf_use = []
            bg[5].append(cb)

            lnt_c = []
            for fc in range(FC):
                t1 = TT("v", tscr, buf[:, fc * CH:(fc + 1) * CH], abuf,
                        ALU.mult, waits=[ca, lt[fc]] + xnbs_use)
                t2 = TT("v", xnbs[:, fc * CH:(fc + 1) * CH], tscr, bbuf,
                        ALU.add, waits=[cb, t1])
                lnt_c.append(t2)
            xnbs_use = []
            for sl in range(P1B_NSL):
                js = 4 * cc + sl
                if P1B_OPS < 2:
                    prev_snap = lnt_c[-1]
                    continue
                mmq = mmk = mmv = None
                for fc in range(FC):
                    xs = xnbs[:, fc * CH + sl * P: fc * CH + sl * P + P]
                    w0 = [lnt_c[fc]] + wt_all + (bg[0] if fc == 0 else [])
                    mmq = MM(PS[0][0:64, 0:P], wq0[:, fc * HD:(fc + 1) * HD], xs,
                             start=(fc == 0), stop=(fc == FC - 1), waits=w0)
                    mmk = MM(PS[0][0:64, P:2 * P], wk0[:, fc * HD:(fc + 1) * HD],
                             xs, start=False, stop=(fc == FC - 1))
                    mmv = MM(PS[0][0:64, 2 * P:3 * P], wv0[:, fc * HD:(fc + 1) * HD],
                             xs, start=False, stop=(fc == FC - 1))
                bg[0] = []
                if P1B_OPS < 3:
                    prev_snap = mmv
                    bg[0] = [mmv]
                    continue
                e1 = pr.emit("v", lambda h: h.tensor_scalar_max(
                    qrel, PS[0][0:64, 0:P], 0.0), waits=[mmv] + small_use,
                    inc="v")
                small_use = []
                e2 = pr.emit("v", lambda h: h.tensor_scalar_min(
                    qmin, PS[0][0:64, 0:P], 0.0), waits=[mmv, elu_prev[0]],
                    inc="v")
                e3 = pr.emit("s", lambda h: h.activation(qexp, qmin, AF.Exp),
                             waits=[e2], inc="s")
                e4 = TT("v", qf_bf, qrel, qexp, ALU.add, waits=[e1, e3])
                f1 = pr.emit("v", lambda h: h.tensor_scalar_max(
                    krel, PS[0][0:64, P:2 * P], 0.0), waits=[mmv], inc="v")
                f2 = pr.emit("v", lambda h: h.tensor_scalar_min(
                    kmin, PS[0][0:64, P:2 * P], 0.0), waits=[mmv, elu_prev[1]],
                    inc="v")
                f3 = pr.emit("s", lambda h: h.activation(kexp, kmin, AF.Exp),
                             waits=[f2], inc="s")
                elu_prev[0] = e3
                elu_prev[1] = f3
                f4 = TT("v", ksum, krel, kexp, ALU.add, waits=[f1, f3])
                f5 = pr.emit("v", lambda h, js=js: h.tensor_scalar_mul(
                    kf_bf, ksum, kfm_sb[:, js:js + 1]), waits=[f4], inc="v")
                g1 = pr.emit("v", lambda h: h.tensor_copy(
                    v_bf, PS[0][0:64, 2 * P:3 * P]), waits=[mmv], inc="v")
                bg[0] = [g1]
                if P1B_OPS < 4:
                    prev_snap = g1
                    continue
                trk = pr.emit("t", lambda h: h.transpose(
                    PSB[:, 0:HD], kf_bf, identb[0:64, 0:64]),
                    waits=[f5] + bg[7], inc="t")
                bg[7] = []
                trv = pr.emit("t", lambda h: h.transpose(
                    PSB[:, HD:2 * HD], v_bf, identb[0:64, 0:64]),
                    waits=[g1], inc="t")
                c1 = pr.emit("v", lambda h: h.tensor_copy(kT_sb, PSB[:, 0:HD]),
                             waits=[trv], inc="v")
                c2 = pr.emit("v", lambda h: h.tensor_copy(
                    vT1_sb[:, 0:HD], PSB[:, HD:2 * HD]),
                    waits=[trv, mz2, mz1c], inc="v")
                bg[7] = [c1, c2]
                if P1B_OPS < 5:
                    prev_snap = c2
                    continue
                at = MM(PS[2][0:P, 0:P], kf_bf, qf_bf,
                        waits=[f5, e4] + bg[2])
                bg[2] = []
                am = TT("v", a_bf, PS[2][0:P, 0:P], utm_sb[:], ALU.mult,
                        waits=[at])
                bg[2] = [am]
                if P1B_OPS < 6:
                    prev_snap = am
                    continue
                n1 = MM(PS[3][0:96, 0:P], vT1_sb, a_bf, start=True, stop=False,
                        waits=[am, c2] + bg[3])
                bg[3] = []
                n2 = MM(PS[3][0:96, 0:P], state_bf, qf_bf, start=False, stop=True,
                        waits=[prev_snap, e4])
                ncp = pr.emit("v", lambda h, js=js: h.tensor_copy(
                    num_sb[:, js * P:(js + 1) * P], PS[3][0:65, 0:P]),
                    waits=[n2], inc="v")
                bg[3] = [ncp]
                numcp_last = ncp
                if P1B_OPS < 7:
                    prev_snap = ncp
                    continue
                su = MM(PS[4][0:64, 0:96], kT_sb, vT1_sb,
                        start=True, stop=True,
                        waits=[c1, c2] + bg[4])
                bg[4] = []
                sacc = pr.emit("v", lambda h: h.tensor_tensor(
                    state_f32, state_f32, PS[4][0:64, 0:96], ALU.add),
                    waits=[su, mz1b, prev_snap], inc="v")
                bg[4] = [sacc]
                snap = pr.emit("v", lambda h: h.tensor_copy(
                    state_bf, state_f32), waits=[sacc], inc="v")
                prev_snap = snap
                if debug and js == 0:
                    pr.dma("g", dbg["d_qf"], qf_bf, waits=[e4])
                    pr.dma("g", dbg["d_kf"], kf_bf, waits=[f5])
                    pr.dma("g", dbg["d_abf"], a_bf, waits=[am])
                    pr.dma("g", dbg["d_vt"], vT1_sb[:, 0:65], waits=[c2])
                    pr.barrier()
                small_use = [n1, n2, su, at, trk, trv, mmq]
                if sl == P1B_NSL - 1:
                    xnbs_use = [mmv]
                p1w_step(3)
                if SAFE:
                    pr.barrier()
            xuse[cc % 2] = lnt_c[-1]
            abuf_use = [lnt_c[-1]]
            bbuf_use = [lnt_c[-1]]

        p1w_flush()
        wdcvt_done = wdcvt_toks[-8:]

        sstore = [None]
        sst_r = Region(A, 2 * CH * 2)
        if not P1B_TAIL:
            pr.barrier()
        if P1B_TAIL:
            dn = rowbuf[0:1, 0:4096]
            dadd = pr.emit("v", lambda h: h.tensor_scalar_add(
                dn, num_sb[64:65, :], 1e-6),
                waits=[numcp_last] + rowg[0] + rowg[1] + rowg[2] + rowg[3] +
                      rowg[4] + rowg[5] + rowg[6] + rowg[7], inc="v")
            for i in range(8):
                rowg[i] = []
            drec = pr.emit("v", lambda h: h.reciprocal(dn, dn), waits=[dadd], inc="v")
            if debug:
                pr.dma("g", dbg["d_num"], num_sb[0:65, 0:S], waits=[numcp_last])
                pr.dma("g", dbg["d_rec"], dn, waits=[drec])
                pr.barrier()
            sstore.clear()
            on_toks = []
            rb_use = []
            for cc in range(NCH):
                bb2 = bcast(5, 0, CH, waits=[drec],
                            rowap=dn[0:1, cc * CH:(cc + 1) * CH])
                rowg[cc % 8].append(bb2) if False else None
                cb2 = pr.emit("v", lambda h: h.tensor_copy(rb_sb, PS[5][0:64, 0:CH]),
                              waits=[bb2] + rb_use, inc="v")
                bg[5].append(cb2)
                on = TT("v", outn_sb[0:64, cc * CH:(cc + 1) * CH],
                        num_sb[0:64, cc * CH:(cc + 1) * CH], rb_sb, ALU.mult,
                        waits=[cb2])
                rb_use = [on]
                on_toks.append(on)
            ev_prev = [None, None]
            for oc in range(FC):
                for cc in range(NCH):
                    bank = 6 if cc % 2 == 0 else 2
                    mo = MM(PS[bank][0:P, 0:CH], wo0[:, oc * P:(oc + 1) * P],
                            outn_sb[0:64, cc * CH:(cc + 1) * CH],
                            waits=[on_toks[cc]] + bg[bank])
                    bg[bank] = []
                    stg = sst_r.bf16(P, CH, byteoff=(cc % 2) * CH * 2)
                    ev = pr.emit("v", lambda h, stg=stg, bank=bank: h.tensor_copy(
                        stg, PS[bank][0:P, 0:CH]),
                        waits=[mo] + (ev_prev[cc % 2] or []), inc="v")
                    bg[bank] = [ev]
                    st = pr.dma("y", s0_scr[oc, :, cc * CH:(cc + 1) * CH], stg,
                                waits=[ev])
                    sstore.append(st)
                    if debug:
                        st2 = pr.dma("g", dbg["d_s0"][oc, :, cc * CH:(cc + 1) * CH],
                                     stg, waits=[ev])
                        sstore.append(st2)
                        ev_prev[cc % 2] = [st, st2]
                    else:
                        ev_prev[cc % 2] = [st, None]

        pr.barrier()
        num_r.free()
        outn_r.free()
        sm.free()
        sst_r.free()
        xr.free()
        sqA_r.free()
        wbr.free()
        wdc_r.free()

        wu_r = Region(A, FC * H0 * 2)
        wu_sb = wu_r.bf16(P, FC * H0)
        wut = [pr.dma("g", wu_sb[:, fc * H0:(fc + 1) * H0], l0_wu[fc])
               for fc in range(FC)]
        wgt = wgt_toks

        x2_r = Region(A, FC * CH * 4)
        s0_r = Region(A, FC * CH * 2)
        h_r = Region(A, (H0 // P) * CH * 2)
        wd_r = Region(A, 2 * CH * 2)
        p2s = Region(A, FC * CH * 2)
        p2sa = Region(A, 8 * 1024)
        p2sb = Region(A, 8 * 1024)
        xnb2 = p2s.bf16(P, FC * CH, byteoff=0)
        abuf2 = p2sa.f32(P, CH, byteoff=0)
        bbuf2 = p2sa.f32(P, CH, byteoff=2048)
        sgbuf = p2sa.f32(P, CH, byteoff=4096)
        ggbuf = p2sa.f32(P, CH, byteoff=6144)
        s0f = p2sb.f32(P, CH, byteoff=0)
        sq2a = p2sb.f32(P, CH, byteoff=2048)
        tres = p2sb.f32(P, CH, byteoff=4096)
        tscr2 = p2sb.f32(P, CH, byteoff=6144)

        xbuf = x2_r.f32(P, FC * CH)
        sbuf0 = s0_r.bf16(P, FC * CH)
        x2use = []
        s0use = []
        wd_use = [[], []]
        ab2_use = []
        x1st_all = []
        hbf_prev = {}
        for cc in range(NCH):
            lt = []
            for fc in range(FC):
                lt.append(pr.dma("y", xbuf[:, fc * CH:(fc + 1) * CH],
                                 x_in[fc, :, cc * CH:(cc + 1) * CH],
                                 waits=x2use))
            x2use = []
            st0 = []
            for fc in range(FC):
                st0.append(pr.dma("s", sbuf0[:, fc * CH:(fc + 1) * CH],
                                  s0_scr[fc, :, cc * CH:(cc + 1) * CH],
                                  waits=s0use + sstore[-16:]))
            s0use = []
            ra = tm_to_row_pe(a0_tm[:, 4 * cc:4 * cc + 4], 4, 0, 5, waits=[])
            rb = tm_to_row_pe(b0_tm[:, 4 * cc:4 * cc + 4], 4, 1, 5, waits=[])
            ba = bcast(5, 0, CH, waits=[ra])
            ca = pr.emit("v", lambda h: h.tensor_copy(abuf2, PS[5][0:P, 0:CH]),
                         waits=[ba] + ab2_use, inc="v")
            bg[5].append(ca)
            bb_ = bcast(5, 1, CH, waits=[rb])
            cb = pr.emit("v", lambda h: h.tensor_copy(bbuf2, PS[5][0:P, 0:CH]),
                         waits=[bb_], inc="v")
            bg[5].append(cb)
            lnt = []
            prev_t2 = None
            for fc in range(FC):
                xsl = xbuf[:, fc * CH:(fc + 1) * CH]
                t1 = TT("v", tscr2, xsl, abuf2, ALU.mult,
                        waits=[ca, prev_t2] + lt)
                t2 = TT("v", xnb2[:, fc * CH:(fc + 1) * CH], tscr2, bbuf2,
                        ALU.add, waits=[cb, t1])
                prev_t2 = t2
                lnt.append(t2)
            ab2_use = [lnt[-1]]
            hw = lnt + wgt
            h_toks = []
            for hc in range(H0 // P):
                bkg, bku = hc % 2, 2 + hc % 2
                mmg = mmu = None
                for fc in range(FC):
                    xs = xnb2[:, fc * CH:(fc + 1) * CH]
                    wgT = wg_sb[:, fc * H0 + hc * P: fc * H0 + (hc + 1) * P]
                    wuT = wu_sb[:, fc * H0 + hc * P: fc * H0 + (hc + 1) * P]
                    mmg = MM(PS[bkg][0:P, 0:CH], wgT, xs, start=(fc == 0),
                             stop=(fc == FC - 1),
                             waits=hw + (bg[bkg] if fc == 0 else []))
                    if fc == 0:
                        bg[bkg] = []
                    mmu = MM(PS[bku][0:P, 0:CH], wuT, xs, start=(fc == 0),
                             stop=(fc == FC - 1),
                             waits=[wut[fc]] + (bg[bku] if fc == 0 else []))
                    if fc == 0:
                        bg[bku] = []
                sg = pr.emit("s", lambda h, bkg=bkg: h.activation(
                    sgbuf, PS[bkg][0:P, 0:CH], AF.Silu, scale=1.0 / 127.0),
                    waits=[mmg, hbf_prev.get("sg")], inc="s")
                bg[bkg] = [sg]
                hbf = TT("v", h_r.bf16(P, CH, byteoff=hc * CH * 2), sgbuf,
                         PS[bku][0:P, 0:CH], ALU.mult,
                         waits=[sg, mmu] + [hbf_prev.get(hc)])
                bg[bku] = [hbf]
                hbf_prev["sg"] = hbf
                h_toks.append(hbf)
                if SAFE2:
                    pr.barrier()
            x1sq_mm = x1sq_mm2 = None
            x1_st = []
            prev_sq2 = sq_carry
            gg_prev2 = [None]
            dgrp_last = {}
            for grp in range(2):
                for hc in range(H0 // P):
                    wdb = wd_r.bf16(P, CH, byteoff=(hc % 2) * CH * 2)
                    wdt = pr.dma("y", wdb,
                                 wd_bf_scr[hc, :, grp * CH:(grp + 1) * CH],
                                 waits=wd_use[hc % 2] + wdcvt_done)
                    wd_use[hc % 2] = []
                    for j in range(4):
                        oc = grp * 4 + j
                        mm_last = MM(PS[j][0:P, 0:CH], wdb[:, j * P:(j + 1) * P],
                                     h_r.bf16(P, CH, byteoff=hc * CH * 2),
                                     start=(hc == 0), stop=(hc == H0 // P - 1),
                                     waits=[wdt, h_toks[hc]] +
                                           (bg[j] if hc == 0 else []))
                        if hc == 0:
                            bg[j] = []
                        dgrp_last[oc] = mm_last
                    wd_use[hc % 2] = [mm_last]
                    if grp == 1:
                        hbf_prev[hc] = mm_last
                    if SAFE2:
                        pr.barrier()
                for j in range(4):
                    oc = grp * 4 + j
                    gg = pr.emit("s", lambda h, j=j: h.activation(
                        ggbuf, PS[j][0:P, 0:CH], AF.Silu, scale=1.0 / 16129.0),
                        waits=[dgrp_last[oc], gg_prev2[0]], inc="s")
                    bg[j] = [gg]
                    s0c = pr.emit("v", lambda h, oc=oc: h.tensor_copy(
                        s0f, sbuf0[:, oc * CH:(oc + 1) * CH]), waits=st0 + [gg],
                        inc="v")
                    t1 = TT("v", tres, s0f, ggbuf, ALU.mult, waits=[s0c])
                    gg_prev2[0] = t1
                    x1sl = xbuf[:, oc * CH:(oc + 1) * CH]
                    t2 = TT("v", x1sl, x1sl, tres, ALU.add, waits=[t1])
                    stx = pr.dma("y", x1_scr[oc, :, cc * CH:(cc + 1) * CH],
                                 x1sl, waits=[t2])
                    x1_st.append(stx)
                    if debug:
                        x1_st.append(pr.dma(
                            "y", dbg["d_x1"][oc, :, cc * CH:(cc + 1) * CH],
                            x1sl, waits=[t2]))
                    sqt = pr.emit("s", lambda h, x1sl=x1sl: h.activation(
                        sq2a, x1sl, AF.Square), waits=[t2, prev_sq2[0]], inc="s")
                    x1sq_mm = MM(PS[6][0:1, 0:CH], ones_col[:], x1sl,
                                 start=(oc == 0), stop=(oc == FC - 1),
                                 waits=[t2] + (bg[6] if oc == 0 else []))
                    if oc == 0:
                        bg[6] = []
                    x1sq_mm2 = MM(PS[4][0:1, 0:CH], ones_col[:], sq2a,
                                  start=(oc == 0), stop=(oc == FC - 1),
                                  waits=[sqt] + (bg[4] if oc == 0 else []))
                    if oc == 0:
                        bg[4] = []
                    prev_sq2[0] = x1sq_mm2
                    if SAFE2:
                        pr.barrier()
            lnu = ln_affine_chunk(6, 4, x1sq_mm, x1sq_mm2, a1_tm, b1_tm, cc)
            if SAFE:
                pr.barrier()
            x2use = [lnu] + x1_st
            s0use = [t1]
            x1st_all = x1_st

        wg_sum = pr.emit("v", lambda h: h.reduce_sum(
            tmsc[:, 16:17], wacc["g"], axis=AX.X), waits=[wsum_done["g"]], inc="v")
        wu_sum = pr.emit("v", lambda h: h.reduce_sum(
            tmsc[:, 17:18], wacc["u"], axis=AX.X), waits=[wsum_done["u"]], inc="v")
        mm_g = MM(PS[5][0:1, 0:1], ones_col[:], tmsc[:, 16:17],
                  waits=[wg_sum] + bg[5])
        bg[5] = []
        mm_u = MM(PS[5][0:1, 1:2], ones_col[:], tmsc[:, 17:18], waits=[wu_sum])
        cpm = pr.emit("v", lambda h: h.tensor_copy(
            rowbuf[0:1, 0:2], PS[5][0:1, 0:2]),
            waits=[mm_g, mm_u] + rowg[0], inc="v")
        rowg[0] = []
        bg[5] = [cpm]
        e1_ = pr.emit("v", lambda h: h.tensor_scalar(
            rowbuf[0:1, 0:2], rowbuf[0:1, 0:2], 1.0 / (H1 * D), 1e-5,
            ALU.mult, ALU.max), waits=[cpm], inc="v")
        e2_ = pr.emit("v", lambda h: h.reciprocal(
            rowbuf[0:1, 3:5], rowbuf[0:1, 0:2]), waits=[e1_], inc="v")
        bcm = bcast(0, 0, 8, waits=[e2_], rowap=rowbuf[0:1, 0:8])
        rowg[0].append(bcm)
        cpm2 = pr.emit("v", lambda h: h.tensor_copy(mcol[:, 0:8], PS[0][0:P, 0:8]),
                       waits=[bcm], inc="v")
        bg[0] = [cpm2]

        pr.barrier()
        wg_r.free()
        wu_r.free()
        x2_r.free()
        s0_r.free()
        h_r.free()
        wd_r.free()
        p2s.free()
        p2sa.free()
        p2sb.free()
        waccr.free()

        xnb_own_r = Region(A, FC * TOWN * 2)
        xq_own_r = Region(A, FC * TOWN * 2)
        ckv_r = Region(A, S * 2)
        ksh_r = [Region(A, S * 2), Region(A, S * 2)]
        qsh_r = [Region(A, TOWN * 2), Region(A, TOWN * 2)]
        xnb_own = xnb_own_r.bf16(P, FC * TOWN)
        xq_own = xq_own_r.bf16(P, FC * TOWN)
        ckv_sb = ckv_r.bf16(P, S)
        kshared = [r.bf16(P, S) for r in ksh_r]
        qshared = [r.bf16(P, TOWN) for r in qsh_r]

        kb1 = pr.dma("g", kshared[0][96:97, :], keybias[:])
        kb2 = pr.dma("g", kshared[1][96:97, :], keybias[:])
        qo1 = pr.emit("v", lambda h: h.memset(qshared[0][96:97, :], 1.0), inc="v")
        qo2 = pr.emit("v", lambda h: h.memset(qshared[1][96:97, :], 1.0), inc="v")

        p3 = Region(A, 56 * 1024)
        x1b = p3.f32(P, FC * CH, byteoff=0)
        xn1 = p3.f32(P, FC * CH, byteoff=FC * CH * 4)
        o3 = 2 * FC * CH * 4
        abuf3 = p3.f32(P, CH, byteoff=o3)
        bbuf3 = p3.f32(P, CH, byteoff=o3 + 2048)
        csb1 = p3.f32(96, CH, byteoff=o3 + 4096)
        csb2 = p3.f32(96, CH, byteoff=o3 + 6144)
        rop1 = p3.f32(96, CH, byteoff=o3 + 8192)
        m8b = p3.f32(P, CH, byteoff=o3 + 10240)
        sxb = p3.f32(P, CH, byteoff=o3 + 12288)
        tq3 = p3.f32(P, CH, byteoff=o3 + 14336)
        wdb3 = p3.f32(P, D, byteoff=o3 + 16384)
        wdacc = wdacc_r.f32(P, NHC1)

        x1use = []
        ab3_use = []
        cs_use = []
        rot_use = []
        wd3_use = []
        wd_sum_last = None
        for cc in range(NCH):
            own = cc >= 4
            lt = []
            for fc in range(FC):
                lt.append(pr.dma("y", x1b[:, fc * CH:(fc + 1) * CH],
                                 x1_scr[fc, :, cc * CH:(cc + 1) * CH],
                                 waits=x1use + [x1st_all[-1]]))
            x1use = []
            ra = tm_to_row_pe(a1_tm[:, 4 * cc:4 * cc + 4], 4, 0, 5, waits=[])
            rb = tm_to_row_pe(b1_tm[:, 4 * cc:4 * cc + 4], 4, 1, 5, waits=[])
            ba = bcast(5, 0, CH, waits=[ra])
            ca = pr.emit("v", lambda h: h.tensor_copy(abuf3, PS[5][0:P, 0:CH]),
                         waits=[ba] + ab3_use, inc="v")
            ab3_use = []
            bg[5].append(ca)
            bb_ = bcast(5, 1, CH, waits=[rb])
            cb = pr.emit("v", lambda h: h.tensor_copy(bbuf3, PS[5][0:P, 0:CH]),
                         waits=[bb_], inc="v")
            bg[5].append(cb)
            lnt = []
            for fc in range(FC):
                xsl = x1b[:, fc * CH:(fc + 1) * CH]
                xnsl = xn1[:, fc * CH:(fc + 1) * CH]
                t1 = TT("v", xnsl, xsl, abuf3, ALU.mult, waits=[ca] + lt)
                t2 = TT("v", xnsl, xnsl, bbuf3, ALU.add, waits=[cb, t1])
                dst = (xnb_own[:, fc * TOWN + (cc - 4) * CH:
                               fc * TOWN + (cc - 3) * CH] if own
                       else xn1[0:0, 0:0])
                if own:
                    t3 = pr.emit("v", lambda h, dst=dst, xnsl=xnsl: h.tensor_copy(
                        dst, xnsl), waits=[t2], inc="v")
                    lnt.append(t3)
                else:
                    lnt.append(t2)
            ab3_use = [lnt[-1]]

            def xnb_src(fc, cc=cc, own=own):
                return None

            xnbc = []
            for fc in range(FC):
                if own:
                    src = xnb_own[:, fc * TOWN + (cc - 4) * CH:
                                  fc * TOWN + (cc - 3) * CH]
                    xnbc.append((src, lnt[fc]))
                else:
                    dst = xq_own[:, fc * TOWN: fc * TOWN + CH]
                    tk = pr.emit("v", lambda h, dst=dst, fc=fc: h.tensor_copy(
                        dst, xn1[:, fc * CH:(fc + 1) * CH]), waits=[lnt[fc]],
                        inc="v")
                    xnbc.append((dst, tk))

            mmc = None
            for fc in range(FC):
                src, tk = xnbc[fc]
                mmc = MM(PS[0][0:P, 0:CH], wkv1[:, fc * DC:(fc + 1) * DC], src,
                         start=(fc == 0), stop=(fc == FC - 1),
                         waits=[tk] + (bg[0] if fc == 0 else []))
                if fc == 0:
                    bg[0] = []
            ckv_cp = pr.emit("v", lambda h, cc=cc: h.tensor_copy(
                ckv_sb[:, cc * CH:(cc + 1) * CH], PS[0][0:P, 0:CH]),
                waits=[mmc], inc="v")
            bg[0] = [ckv_cp]
            c1t = pr.dma("y", csb1[64:96, :], cs1k[:, cc * CH:(cc + 1) * CH],
                         waits=cs_use)
            c2t = pr.dma("y", csb2[64:96, :], cs2k[:, cc * CH:(cc + 1) * CH])
            cs_use = []
            mk1 = MM(PS[6][64:96, 0:CH], wkr1[:], ckv_sb[:, cc * CH:(cc + 1) * CH],
                     waits=[ckv_cp] + bg[6], tile_position=(0, 64))
            bg[6] = []
            mk2 = MM(PS[2][64:96, 0:CH], wkr2[:], ckv_sb[:, cc * CH:(cc + 1) * CH],
                     waits=[ckv_cp] + bg[2], tile_position=(0, 64))
            bg[2] = []
            r1 = TT("v", rop1[64:96, :], PS[6][64:96, 0:CH], csb1[64:96, :],
                    ALU.mult, waits=[mk1, c1t] + rot_use)
            rot_use = []
            bg[6] = [r1]
            tkr = TT("v", csb1[64:96, :], PS[2][64:96, 0:CH], csb2[64:96, :],
                     ALU.mult, waits=[mk2, c2t, r1])
            bg[2] = [tkr]
            rot = TT("v", kshared[0][64:96, cc * CH:(cc + 1) * CH],
                     rop1[64:96, :], csb1[64:96, :], ALU.add, waits=[tkr])
            rot2 = pr.emit("v", lambda h, cc=cc: h.tensor_copy(
                kshared[1][64:96, cc * CH:(cc + 1) * CH],
                kshared[0][64:96, cc * CH:(cc + 1) * CH]), waits=[rot], inc="v")
            cs_use = [rot]
            rot_use = [rot]
            last_tok = rot2
            if own:
                oc4 = cc - 4
                c3t = pr.dma("y", csb1[64:96, :], cs1q[:, oc4 * CH:(oc4 + 1) * CH],
                             waits=[rot])
                c4t = pr.dma("y", csb2[64:96, :], cs2q[:, oc4 * CH:(oc4 + 1) * CH],
                             waits=[tkr])
                cs_use = []
                mq1 = mq2 = None
                for fc in range(FC):
                    src, tk = xnbc[fc]
                    mq1 = MM(PS[6][64:96, 0:CH], wqr1[:, fc * DR:(fc + 1) * DR],
                             src, start=(fc == 0), stop=(fc == FC - 1),
                             waits=[tk] + (bg[6] if fc == 0 else []),
                             tile_position=(0, 64))
                    if fc == 0:
                        bg[6] = []
                    mq2 = MM(PS[2][64:96, 0:CH], wqr2[:, fc * DR:(fc + 1) * DR],
                             src, start=(fc == 0), stop=(fc == FC - 1),
                             waits=(bg[2] if fc == 0 else []),
                             tile_position=(0, 64))
                    if fc == 0:
                        bg[2] = []
                r1q = TT("v", rop1[64:96, :], PS[6][64:96, 0:CH], csb1[64:96, :],
                         ALU.mult, waits=[mq1, c3t])
                bg[6] = [r1q]
                t2q = TT("v", csb1[64:96, :], PS[2][64:96, 0:CH], csb2[64:96, :],
                         ALU.mult, waits=[mq2, c4t, r1q])
                bg[2] = [t2q]
                rq = TT("v", qshared[0][64:96, oc4 * CH:(oc4 + 1) * CH],
                        rop1[64:96, :], csb1[64:96, :], ALU.add, waits=[t2q])
                rq2 = pr.emit("v", lambda h, oc4=oc4: h.tensor_copy(
                    qshared[1][64:96, oc4 * CH:(oc4 + 1) * CH],
                    qshared[0][64:96, oc4 * CH:(oc4 + 1) * CH]),
                    waits=[rq], inc="v")
                cs_use = [rq]
                rot_use = [rq]
                mm8 = pr.emit("v", lambda h: h.reduce_max(
                    m8b, xn1.rearrange("p (c n) -> p n c", c=FC), axis=AX.X,
                    apply_absolute_value=True), waits=lnt, inc="v")
                trts = []
                for q4 in range(4):
                    trt = pr.emit("t", lambda h, q4=q4: h.transpose(
                        PS[3][0:P, 0:P], m8b[:, q4 * P:(q4 + 1) * P], ident32[:]),
                        waits=[mm8] + bg[3], inc="t")
                    bg[3] = []
                    rmx = pr.emit("v", lambda h, q4=q4: h.reduce_max(
                        tmsc[:, 20 + q4:21 + q4], PS[3][0:P, 0:P], axis=AX.X),
                        waits=[trt], inc="v")
                    bg[3] = [rmx]
                    trts.append(rmx)
                aq1 = pr.emit("v", lambda h: h.tensor_scalar_max(
                    tmsc[:, 20:24], tmsc[:, 20:24], 1e-5), waits=trts, inc="v")
                aq2 = pr.emit("v", lambda h, oc4=oc4: h.tensor_scalar_mul(
                    isx_tm[:, oc4 * 4:(oc4 + 1) * 4], tmsc[:, 20:24],
                    1.0 / 127.0), waits=[aq1], inc="v")
                aq3 = pr.emit("v", lambda h: h.reciprocal(
                    tmsc[:, 24:28], tmsc[:, 20:24]), waits=[aq2], inc="v")
                aq4 = pr.emit("v", lambda h: h.tensor_scalar_mul(
                    tmsc[:, 24:28], tmsc[:, 24:28], 127.0), waits=[aq3], inc="v")
                sxr = tm_to_row_pe(tmsc[:, 24:28], 4, 2, 4, waits=[aq4])
                bsx = bcast(4, 2, CH, waits=[sxr])
                csx = pr.emit("v", lambda h: h.tensor_copy(sxb, PS[4][0:P, 0:CH]),
                              waits=[bsx], inc="v")
                bg[4] = [csx]
                qts = []
                prev_qt = None
                for fc in range(FC):
                    xnsl = xn1[:, fc * CH:(fc + 1) * CH]
                    tqs = TT("v", tq3, xnsl, sxb, ALU.mult, waits=[csx, prev_qt])
                    qt = pr.emit("v", lambda h, fc=fc, oc4=oc4: h.tensor_scalar(
                        xq_own[:, fc * TOWN + oc4 * CH:
                               fc * TOWN + (oc4 + 1) * CH],
                        tq3, RC, RC, ALU.add, ALU.subtract), waits=[tqs], inc="v")
                    prev_qt = qt
                    qts.append(qt)
                last_tok = qts[-1]
            x1use = [last_tok]
            for k in range(4):
                hc = cc * 4 + k
                wdt = pr.dma("s", wdb3, l1_wd[hc], waits=wd3_use)
                rt = pr.emit("v", lambda h, hc=hc: h.reduce_sum(
                    wdacc[:, hc:hc + 1], wdb3, axis=AX.X,
                    apply_absolute_value=True), waits=[wdt], inc="v")
                wd3_use = [rt]
                wd_sum_last = rt

        wd_rs = pr.emit("v", lambda h: h.reduce_sum(
            tmsc[:, 18:19], wdacc[:], axis=AX.X), waits=[wd_sum_last], inc="v")
        mmd = MM(PS[5][0:1, 0:1], ones_col[:], tmsc[:, 18:19],
                 waits=[wd_rs] + bg[5])
        bg[5] = []
        cpd = pr.emit("v", lambda h: h.tensor_copy(
            rowbuf[0:1, 8:9], PS[5][0:1, 0:1]), waits=[mmd] + rowg[0], inc="v")
        rowg[0] = []
        bg[5] = [cpd]
        e1_ = pr.emit("v", lambda h: h.tensor_scalar(
            rowbuf[0:1, 8:9], rowbuf[0:1, 8:9], 1.0 / (H1 * D), 1e-5,
            ALU.mult, ALU.max), waits=[cpd], inc="v")
        e2_ = pr.emit("v", lambda h: h.reciprocal(
            rowbuf[0:1, 9:10], rowbuf[0:1, 8:9]), waits=[e1_], inc="v")
        bcd = bcast(5, 0, 2, waits=[e2_], rowap=rowbuf[0:1, 8:10])
        rowg[0].append(bcd)
        cpd2 = pr.emit("v", lambda h: h.tensor_copy(
            tmsc[:, 30:32], PS[5][0:P, 0:2]), waits=[bcd], inc="v")
        bg[5] = [cpd2]
        mde = pr.emit("v", lambda h: h.tensor_copy(mcol[:, 2:3], tmsc[:, 30:31]),
                      waits=[cpd2], inc="v")
        swd = pr.emit("v", lambda h: h.tensor_copy(mcol[:, 5:6], tmsc[:, 31:32]),
                      waits=[cpd2], inc="v")

        if debug:
            pr.barrier()
            pr.dma("g", dbg["d_ckv"], ckv_sb[:])
            pr.dma("y", dbg["d_mw"], mcol[:])
            pr.dma("y", dbg["d_isx"], isx_tm[:])
            for fc in range(FC):
                pr.dma("g", dbg["d_xq"][fc], xq_own[:, fc * TOWN:(fc + 1) * TOWN])
            pr.dma("y", dbg["d_ab"][0], a0_tm[:])
            pr.dma("y", dbg["d_ab"][1], b0_tm[:])
            pr.dma("y", dbg["d_ab"][2], a1_tm[:])
            pr.dma("y", dbg["d_ab"][3], b1_tm[:])
            pr.barrier()

        pr.barrier()
        p3.free()
        wdacc_r.free()

        vt_r = Region(A, NSLOT * 130 * 2)
        attnr_r = Region(A, 1024 * 4 + 256)
        attnf_pk = Region(A, (NH // 2) * TOWN * 2)

        def attnf_ap(h_, c0, c1):
            return attnf_pk.bf16r(64 * (h_ % 2), 64, c1 - c0,
                                  byteoff=(h_ // 2) * TOWN * 2 + c0 * 2)
        pbuf_r = Region(A, 2 * 1024 * 2)
        rb3_r = Region(A, CH * 4)
        wqh_r = Region(A, 2 * CH * 2)
        tern_r = Region(A, 2 * (D * 4 + D * 2))
        dm_r = Region(A, 4 * CH * 2)
        dmask_sb = dm_r.bf16(P, 4 * CH)
        dmt = pr.dma("g", dmask_sb.rearrange("p (r n) -> p r n", r=4),
                     dmask.transpose([1, 0, 2]))

        vtaug = vt_r.bf16(P, NSLOT * 130)
        attnraw = attnr_r.f32(65, 1024)
        rb3 = rb3_r.f32(64, CH)

        tern_jobs = []
        for m, src, dst, swcol in (("g", l1_wg, tern_g, 3), ("u", l1_wu, tern_u, 4),
                                   ("d", l1_wd, tern_d, 5)):
            for hc in range(NHC1):
                tern_jobs.append((src, dst, swcol, hc))
        tern_use = [[], []]
        tern_idx = [0]
        tern_stores = []

        def emit_tern(njobs):
            for _ in range(njobs):
                if tern_idx[0] >= len(tern_jobs):
                    return
                src, dst, swcol, hc = tern_jobs[tern_idx[0]]
                i = tern_idx[0] % 2
                tern_idx[0] += 1
                win = tern_r.f32(P, D, byteoff=i * (D * 4 + D * 2))
                wout = tern_r.bf16(P, D, byteoff=i * (D * 4 + D * 2) + D * 4)
                wtk = pr.dma("s", win, src[hc], waits=tern_use[i])
                tern_use[i] = []
                sw = mcol[:, swcol:swcol + 1]
                t1 = pr.emit("v", lambda h, win=win, sw=sw: h.tensor_scalar(
                    win, win, sw, RC, ALU.mult, ALU.add),
                    waits=[wtk, cpm2, swd], inc="v")
                t2 = pr.emit("v", lambda h, win=win: h.tensor_scalar(
                    win, win, RC, -1.0, ALU.subtract, ALU.max), waits=[t1],
                    inc="v")
                t3 = pr.emit("v", lambda h, win=win, wout=wout: h.tensor_scalar_min(
                    wout, win, 1.0), waits=[t2], inc="v")
                stk = pr.dma("g", dst[hc], wout, waits=[t3])
                tern_use[i] = [stk]
                tern_stores.append(stk)

        def kt_count(qc):
            return 16 + 4 * qc + 4

        kshared_use = [[kb1], [kb2]]
        qshared_use = [[qo1], [qo2]]
        wqh_use = [[], []]
        vt_use = []
        pbuf_use = [[], []]
        attnf_toks = []
        attnf_done = {}
        for hg in range(NH // 2):
            mzv = pr.emit("v", lambda h: h.memset(vtaug[:], 1.0),
                          waits=vt_use, inc="v")
            vt_use = []
            vt_toks = []
            for tt in range(NSLOT):
                mmv = MM(PS[6][0:P, 0:P], ckv_sb[:, tt * P:(tt + 1) * P],
                         wvup1[:, hg * 128:(hg + 1) * 128],
                         waits=bg[6] + [mzv])
                bg[6] = []
                out_ap = (vtaug[:, tt * 130:(tt + 1) * 130]
                          .rearrange("p (g n) -> p g n", g=2)[:, :, 0:64])
                ev = pr.emit("v", lambda h, out_ap=out_ap: h.tensor_copy(
                    out_ap, PS[6][0:P, 0:P].rearrange("p (g n) -> p g n", g=2)),
                    waits=[mmv], inc="v")
                bg[6] = [ev]
                vt_toks.append(ev)
            vt_done = vt_toks[-1]
            for hh in range(2):
                h_ = 2 * hg + hh
                hb = h_ % 2
                kev = []
                for cc in range(NCH):
                    mmk = MM(PS[0][0:64, 0:CH], wkup1[:, h_ * HD:(h_ + 1) * HD],
                             ckv_sb[:, cc * CH:(cc + 1) * CH],
                             waits=kshared_use[hb] + bg[0])
                    bg[0] = []
                    ev = pr.emit("v", lambda h, cc=cc, hb=hb: h.tensor_copy(
                        kshared[hb][0:64, cc * CH:(cc + 1) * CH],
                        PS[0][0:64, 0:CH]), waits=[mmk], inc="v")
                    bg[0] = [ev]
                    kev.append(ev)
                kshared_use[hb] = []
                wqh = wqh_r.bf16(P, CH, byteoff=hb * CH * 2)
                wqt = pr.dma("g", wqh.rearrange("p (c k) -> p c k", c=FC),
                             l1_wq[:, :, h_ * HD:(h_ + 1) * HD].transpose([1, 0, 2]),
                             waits=wqh_use[hb])
                wqh_use[hb] = []
                wqs = pr.emit("v", lambda h, wqh=wqh: h.tensor_scalar_mul(
                    wqh, wqh, INV_SQRT96), waits=[wqt], inc="v")
                qev = []
                mmq = None
                for tc in range(4):
                    for fc in range(FC):
                        mmq = MM(PS[0][0:64, 0:CH], wqh[:, fc * HD:(fc + 1) * HD],
                                 xnb_own[:, fc * TOWN + tc * CH:
                                         fc * TOWN + (tc + 1) * CH],
                                 start=(fc == 0), stop=(fc == FC - 1),
                                 waits=[wqs] + qshared_use[hb] +
                                       (bg[0] if fc == 0 else []))
                        if fc == 0:
                            bg[0] = []
                    ev = pr.emit("v", lambda h, tc=tc, hb=hb: h.tensor_copy(
                        qshared[hb][0:64, tc * CH:(tc + 1) * CH],
                        PS[0][0:64, 0:CH]), waits=[mmq], inc="v")
                    bg[0] = [ev]
                    qev.append(ev)
                qshared_use[hb] = []
                wqh_use[hb] = [mmq]
                emit_tern(3)
                kq_ready = [kev[-1], qev[-1]]
                s_mms = []
                for qp in range(2):
                    qc0, qc1 = 2 * qp, 2 * qp + 1
                    pvb0, pvb1 = 1, 6
                    pv_last0 = pv_last1 = None
                    pend = None
                    for kt in range(kt_count(qc1) + 1):
                        cur = None
                        if kt < kt_count(qc1):
                            in_qc0 = kt < kt_count(qc0)
                            sb0 = 2 + 2 * (kt % 2)
                            sb1_ = sb0 + 1
                            lhs = kshared[hb][0:97, kt * P:(kt + 1) * P]
                            pb = pbuf_r.bf16(P, 1024, byteoff=(kt % 2) * 1024 * 2)
                            wide = PSW[kt % 2]
                            smm0 = None
                            if in_qc0:
                                smm0 = MM(PS[sb0][0:P, 0:CH], lhs,
                                          qshared[hb][0:97,
                                                      qc0 * CH:(qc0 + 1) * CH],
                                          waits=kq_ready + bg[sb0] + bg[sb1_])
                                bg[sb0] = []
                            smm1 = MM(PS[sb1_][0:P, 0:CH], lhs,
                                      qshared[hb][0:97, qc1 * CH:(qc1 + 1) * CH],
                                      waits=kq_ready + bg[sb1_] + bg[sb0])
                            bg[sb0] = []
                            bg[sb1_] = []
                            s_mms.append(smm1)
                            if in_qc0:
                                ex1 = pr.emit("s", lambda h, pb=pb, wide=wide:
                                              h.activation(pb[:, 0:1024],
                                                           wide[0:P, 0:1024],
                                                           AF.Exp),
                                              waits=[smm0, smm1] +
                                                    pbuf_use[kt % 2], inc="s")
                                ex0 = ex1
                            else:
                                ex1 = pr.emit("s", lambda h, pb=pb, sb1_=sb1_:
                                              h.activation(pb[:, CH:1024],
                                                           PS[sb1_][0:P, 0:CH],
                                                           AF.Exp),
                                              waits=[smm1] + pbuf_use[kt % 2],
                                              inc="s")
                                ex0 = None
                            bg[sb0] = [ex1]
                            bg[sb1_] = [ex1]
                            pbuf_use[kt % 2] = []
                            mts = []
                            if kt >= 16 and (kt - 16) // 4 == qc0 and in_qc0:
                                r = (kt - 16) % 4
                                mt = pr.emit(
                                    "g", lambda h, pb=pb, r=r: h.tensor_tensor(
                                        pb[:, 0:CH], pb[:, 0:CH],
                                        dmask_sb[:, r * CH:(r + 1) * CH],
                                        ALU.mult),
                                    waits=[ex0, dmt], inc="g")
                                mts.append(mt)
                            if kt >= 16 and (kt - 16) // 4 == qc1:
                                r = (kt - 16) % 4
                                mt = pr.emit(
                                    "g", lambda h, pb=pb, r=r: h.tensor_tensor(
                                        pb[:, CH:1024], pb[:, CH:1024],
                                        dmask_sb[:, r * CH:(r + 1) * CH],
                                        ALU.mult),
                                    waits=[ex1, dmt], inc="g")
                                mts.append(mt)
                            cur = (kt, in_qc0, ex0, ex1, mts, pb)
                        if pend is not None:
                            pkt, p_in0, pex0, pex1, pmts, ppb = pend
                            vt_l = vtaug[:, pkt * 130 + hh * 65:
                                         pkt * 130 + hh * 65 + 65]
                            if p_in0:
                                pv_last0 = MM(PS[pvb0][0:65, 0:CH], vt_l,
                                              ppb[:, 0:CH],
                                              start=(pkt == 0),
                                              stop=(pkt == kt_count(qc0) - 1),
                                              waits=[pex0, vt_done] + pmts +
                                                    (bg[pvb0] if pkt == 0
                                                     else []))
                                if pkt == 0:
                                    bg[pvb0] = []
                                pbuf_use[pkt % 2].append(pv_last0)
                            pv_last1 = MM(PS[pvb1][0:65, 0:CH], vt_l,
                                          ppb[:, CH:1024],
                                          start=(pkt == 0),
                                          stop=(pkt == kt_count(qc1) - 1),
                                          waits=[pex1, vt_done] + pmts +
                                                (bg[pvb1] if pkt == 0 else []))
                            if pkt == 0:
                                bg[pvb1] = []
                            pbuf_use[pkt % 2].append(pv_last1)
                        pend = cur
                    ev0 = pr.emit("v", lambda h: h.tensor_copy(
                        attnraw[:, 0:CH], PS[pvb0][0:65, 0:CH]),
                        waits=[pv_last0] + attnf_toks[-1:], inc="v")
                    bg[pvb0] = [ev0]
                    ev1 = pr.emit("v", lambda h: h.tensor_copy(
                        attnraw[:, CH:1024], PS[pvb1][0:65, 0:CH]),
                        waits=[pv_last1], inc="v")
                    bg[pvb1] = [ev1]
                    d3 = pr.emit("v", lambda h: h.reciprocal(
                        row(2, 1024), attnraw[64:65, 0:1024]),
                        waits=[ev0, ev1] + rowg[2] + rowg[3], inc="v")
                    rowg[2] = []
                    rowg[3] = []
                    for (qq, half) in ((qc0, 0), (qc1, 1)):
                        bb3 = bcast(2, 2, CH, waits=[d3],
                                    rowap=row(2, 1024)[0:1, half * CH:
                                                       (half + 1) * CH])
                        rowg[2].append(bb3)
                        rowg[3].append(bb3)
                        cb3 = pr.emit("v", lambda h: h.tensor_copy(
                            rb3, PS[2][0:64, 0:CH]),
                            waits=[bb3] + attnf_toks[-1:], inc="v")
                        bg[2] = [cb3]
                        nt = TT("v", attnf_ap(h_, qq * CH, (qq + 1) * CH),
                                attnraw[0:64, half * CH:(half + 1) * CH], rb3,
                                ALU.mult, waits=[cb3])
                        attnf_toks.append(nt)
                attnf_done[h_] = attnf_toks[-1]
                kshared_use[hb] = s_mms[-1:]
                qshared_use[hb] = s_mms[-1:]
                vt_use = [pv_last1]
                emit_tern(3)

        pr.barrier()
        vt_r.free()
        attnr_r.free()
        dm_r.free()
        pbuf_r.free()
        wqh_r.free()
        xnb_own_r.free()
        ckv_r.free()
        tern_r.free()
        rb3_r.free()
        for r_ in ksh_r:
            r_.free()
        for r_ in qsh_r:
            r_.free()
        wo1_r = Region(A, NH * D * 2)
        s1_r = [Region(A, FC // 2 * TOWN * 2) for _ in range(2)]

        def s1_ap(oc, c0, c1):
            reg = s1_r[oc // 4]
            base = (oc % 4) * TOWN
            return reg.bf16(P, FC // 2 * TOWN)[:, base + c0:base + c1]

        wot = [pr.dma("g", wo1_r.bf16(HD, NH * D), l1_wo[:])]

        def wo1_ap(h_, c0, c1):
            return wo1_r.bf16(HD, NH * D)[:, c0:c1]

        odd_r = Region(A, (NH // 2) * CH * 2)
        odd_st = odd_r.bf16(64, (NH // 2) * CH)
        odd_use = []
        s1_toks = []
        for tc in range(4):
            ost = {}
            for k, h_ in enumerate(range(1, NH, 2)):
                ost[h_] = pr.emit(
                    "v", lambda h, k=k, h_=h_: h.tensor_copy(
                        odd_st[:, k * CH:(k + 1) * CH],
                        attnf_ap(h_, tc * CH, (tc + 1) * CH)),
                    waits=[attnf_done[h_]] + (odd_use if k == 0 else []),
                    inc="v")
            odd_use = []
            for oc in range(FC):
                bank = 0 if oc % 2 == 0 else 2
                mo = None
                for h_ in range(NH):
                    rhs = (attnf_ap(h_, tc * CH, (tc + 1) * CH)
                           if h_ % 2 == 0 else
                           odd_st[:, (h_ // 2) * CH:(h_ // 2 + 1) * CH])
                    mo = MM(PS[bank][0:P, 0:CH],
                            wo1_ap(h_, h_ * D + oc * P, h_ * D + (oc + 1) * P),
                            rhs, start=(h_ == 0), stop=(h_ == NH - 1),
                            waits=wot + [attnf_done[h_], ost[h_ | 1]] +
                                  (bg[bank] if h_ == 0 else []))
                    if h_ == 0:
                        bg[bank] = []
                ev = pr.emit("v", lambda h, oc=oc, tc=tc, bank=bank:
                             h.tensor_copy(
                                 s1_ap(oc, tc * CH, (tc + 1) * CH),
                                 PS[bank][0:P, 0:CH]), waits=[mo], inc="v")
                bg[bank] = [ev]
                odd_use.append(mo)
                s1_toks.append(ev)
        if debug:
            for fc in range(FC):
                pr.dma("g", dbg["d_s1"][fc], s1_ap(fc, 0, TOWN),
                       waits=s1_toks[-1:])

        pr.barrier()
        attnf_pk.free()
        wo1_r.free()
        odd_r.free()

        gu_r = Region(A, NHC1 * CH * 4)
        x1o_r = Region(A, FC * CH * 4)
        tw_r = Region(A, 2 * 2 * D * 2)
        p4 = Region(A, 8 * 1024)
        p4b = Region(A, 8 * 1024)
        isxb = p4.f32(P, CH, byteoff=0)
        sgub = p4.f32(P, CH, byteoff=2048)
        adb = p4.f32(P, CH, byteoff=4096)
        tg4 = p4.f32(P, CH, byteoff=6144)
        sg4 = p4b.f32(P, CH, byteoff=0)
        gg4 = p4b.f32(P, CH, byteoff=2048)
        s1f4 = p4b.f32(P, CH, byteoff=4096)
        m84 = p4b.f32(P, CH, byteoff=6144)

        p4c = Region(A, 4 * 1024)
        mxb = p4c.f32(P, CH, byteoff=0)
        mnb = p4c.f32(P, CH, byteoff=2048)
        gu = gu_r.f32(P, NHC1 * CH)
        xqd = gu_r.bf16(P, NHC1 * CH)
        x1o = x1o_r.f32(P, FC * CH)

        tw_use = [[], []]
        x1o_use = []
        isxb_use = []
        ystores = []
        for tc in range(4):
            cc = tc + 4
            rs = tm_to_row_pe(isx_tm[:, tc * 4:(tc + 1) * 4], 4, 0, 5, waits=[])
            bi = bcast(5, 0, CH, waits=[rs])
            ci = pr.emit("v", lambda h: h.tensor_copy(isxb, PS[5][0:P, 0:CH]),
                         waits=[bi] + isxb_use, inc="v")
            bg[5] = [ci]
            lt = []
            for fc in range(FC):
                lt.append(pr.dma("y", x1o[:, fc * CH:(fc + 1) * CH],
                                 x1_scr[fc, :, cc * CH:(cc + 1) * CH],
                                 waits=x1o_use))
            x1o_use = []
            gu_toks = []
            tg4_use = []
            mm_prev = trts[-1:] if tc > 0 else []
            mz_x = pr.emit("v", lambda h: h.memset(mxb, 0.0), waits=mm_prev,
                           inc="v")
            mz_n = pr.emit("v", lambda h: h.memset(mnb, 0.0), inc="v")
            mm_carry = [mz_x, mz_n]
            for hc in range(NHC1):
                i = hc % 2
                twg = tw_r.bf16(P, D, byteoff=i * 2 * D * 2)
                twu = tw_r.bf16(P, D, byteoff=i * 2 * D * 2 + D * 2)
                tg_t = pr.dma("y", twg, tern_g[hc],
                              waits=tw_use[i] + tern_stores[-8:])
                tu_t = pr.dma("y", twu, tern_u[hc], waits=[])
                tw_use[i] = []
                bkg, bku = hc % 2, 2 + hc % 2
                mmg = mmu = None
                for fc in range(FC):
                    xqs = xq_own[:, fc * TOWN + tc * CH: fc * TOWN + (tc + 1) * CH]
                    mmg = MM(PS[bkg][0:P, 0:CH], twg[:, fc * P:(fc + 1) * P],
                             xqs, start=(fc == 0), stop=(fc == FC - 1),
                             waits=[tg_t] + (bg[bkg] if fc == 0 else []))
                    if fc == 0:
                        bg[bkg] = []
                    mmu = MM(PS[bku][0:P, 0:CH], twu[:, fc * P:(fc + 1) * P],
                             xqs, start=(fc == 0), stop=(fc == FC - 1),
                             waits=[tu_t] + (bg[bku] if fc == 0 else []))
                    if fc == 0:
                        bg[bku] = []
                t1 = TT("v", tg4, PS[bkg][0:P, 0:CH], isxb, ALU.mult,
                        waits=[mmg, ci] + tg4_use)
                bg[bkg] = [t1]
                sg_ = pr.emit("s", lambda h: h.activation(
                    sg4, tg4, AF.Silu, scale=mcol[:, 0:1]),
                    waits=[t1, cpm2], inc="s")
                gut = TT("v", gu[:, hc * CH:(hc + 1) * CH], sg4,
                         PS[bku][0:P, 0:CH], ALU.mult, waits=[sg_, mmu])
                bg[bku] = [gut]
                tg4_use = [sg_]
                gu_toks.append(gut)
                tw_use[i] = [mmg, mmu]
                mxt = TT("v", mxb, mxb, gu[:, hc * CH:(hc + 1) * CH], ALU.max,
                         waits=[gut] + mm_carry)
                mnt = TT("v", mnb, mnb, gu[:, hc * CH:(hc + 1) * CH], ALU.min,
                         waits=[gut])
                mm_carry = []
            mneg = pr.emit("v", lambda h: h.tensor_scalar_mul(
                mnb, mnb, -1.0), waits=[mnt], inc="v")
            mm8 = TT("v", m84, mxb, mnb, ALU.max, waits=[mxt, mneg])
            trts = []
            for q4 in range(4):
                trt = pr.emit("t", lambda h, q4=q4: h.transpose(
                    PS[6][0:P, 0:P], m84[:, q4 * P:(q4 + 1) * P], ident32[:]),
                    waits=[mm8] + bg[6], inc="t")
                bg[6] = []
                rmx = pr.emit("v", lambda h, q4=q4: h.reduce_max(
                    tmsc[:, 20 + q4:21 + q4], PS[6][0:P, 0:P], axis=AX.X),
                    waits=[trt], inc="v")
                bg[6] = [rmx]
                trts.append(rmx)
            a1_ = pr.emit("v", lambda h: h.tensor_scalar_max(
                tmsc[:, 20:24], tmsc[:, 20:24], 1e-5), waits=trts, inc="v")
            a2_ = pr.emit("v", lambda h: h.reciprocal(
                tmsc[:, 24:28], tmsc[:, 20:24]), waits=[a1_], inc="v")
            a3_ = pr.emit("v", lambda h: h.tensor_scalar_mul(
                tmsc[:, 24:28], tmsc[:, 24:28], 127.0), waits=[a2_], inc="v")
            a4_ = pr.emit("v", lambda h: h.tensor_scalar_mul(
                tmsc[:, 28:32], tmsc[:, 20:24], 1.0 / 127.0), waits=[a3_], inc="v")
            a5_ = pr.emit("v", lambda h: h.tensor_scalar_mul(
                tmsc[:, 28:32], tmsc[:, 28:32], mcol[:, 1:2]), waits=[a4_], inc="v")
            a6_ = pr.emit("v", lambda h: h.tensor_scalar_mul(
                tmsc[:, 28:32], tmsc[:, 28:32], mcol[:, 2:3]),
                waits=[a5_, mde], inc="v")
            r1_ = tm_to_row_pe(tmsc[:, 24:28], 4, 2, 5, waits=[a6_])
            r2_ = tm_to_row_pe(tmsc[:, 28:32], 4, 3, 5, waits=[a6_])
            rix = tm_to_row_pe(isx_tm[:, tc * 4:(tc + 1) * 4], 4, 1, 5, waits=[])
            adr = pr.emit("v", lambda h: h.tensor_tensor(
                row(3), row(3), row(1), ALU.mult), waits=[r2_, rix], inc="v")
            rowg[1].append(adr)
            rowg[3].append(adr)
            bs1 = bcast(5, 2, CH, waits=[r1_])
            cs1_ = pr.emit("v", lambda h: h.tensor_copy(sgub, PS[5][0:P, 0:CH]),
                           waits=[bs1], inc="v")
            bg[5] = [cs1_]
            bs2 = bcast(5, 3, CH, waits=[adr])
            cs2_ = pr.emit("v", lambda h: h.tensor_copy(adb, PS[5][0:P, 0:CH]),
                           waits=[bs2], inc="v")
            bg[5] = [cs2_]
            xqd_toks = []
            tq_use = []
            for hc in range(NHC1):
                t1 = TT("v", tg4, gu[:, hc * CH:(hc + 1) * CH], sgub, ALU.mult,
                        waits=[cs1_] + tq_use)
                t2 = pr.emit("v", lambda h, hc=hc: h.tensor_scalar(
                    xqd[:, hc * CH:(hc + 1) * CH], tg4, RC, RC,
                    ALU.add, ALU.subtract), waits=[t1], inc="v")
                tq_use = [t2]
                xqd_toks.append(t2)
            gg_prev = []
            for grp in range(2):
                dmm = [None] * 4
                for hc in range(NHC1):
                    i = hc % 2
                    twd = tw_r.bf16(P, CH, byteoff=i * 2 * D * 2)
                    td_t = pr.dma("y", twd,
                                  tern_d[hc, :, grp * CH:(grp + 1) * CH],
                                  waits=tw_use[i])
                    tw_use[i] = []
                    for j in range(4):
                        dmm[j] = MM(PS[j][0:P, 0:CH], twd[:, j * P:(j + 1) * P],
                                    xqd[:, hc * CH:(hc + 1) * CH],
                                    start=(hc == 0), stop=(hc == NHC1 - 1),
                                    waits=[td_t, xqd_toks[hc]] +
                                          (bg[j] if hc == 0 else []))
                        if hc == 0:
                            bg[j] = []
                    tw_use[i] = [dmm[3]]
                for j in range(4):
                    oc = grp * 4 + j
                    t1 = TT("v", tg4, PS[j][0:P, 0:CH], adb, ALU.mult,
                            waits=[dmm[j], cs2_] + tg4_use)
                    bg[j] = [t1]
                    tg4_use = []
                    g2 = pr.emit("s", lambda h: h.activation(gg4, tg4, AF.Silu),
                                 waits=[t1] + gg_prev, inc="s")
                    s1c = pr.emit("v", lambda h, oc=oc, tc=tc: h.tensor_copy(
                        s1f4, s1_ap(oc, tc * CH, (tc + 1) * CH)),
                        waits=[g2] + s1_toks[-1:], inc="v")
                    t2 = TT("v", gg4, gg4, s1f4, ALU.mult, waits=[s1c])
                    gg_prev = [t2]
                    x1sl = x1o[:, oc * CH:(oc + 1) * CH]
                    t3 = TT("v", x1sl, x1sl, gg4, ALU.add, waits=[t2] + lt)
                    yst = pr.dma("y", y_out[oc, :, tc * CH:(tc + 1) * CH],
                                 x1sl, waits=[t3])
                    ystores.append(yst)
                    tg4_use = [g2]
            x1o_use = ystores[-FC:]
            isxb_use = [t1]

        pr.emit("y", None, waits=ystores[-1:])
        pr.barrier()

        pr.replay(block)
    return nc


def _host_inputs(ins, core):
    b, h = core // 2, core % 2
    x = np.asarray(ins["L0_attn_x"][b], dtype=np.float32)
    own = slice(h * TOWN, (h + 1) * TOWN)
    oth = slice((1 - h) * TOWN, (2 - h) * TOWN)
    xo = np.concatenate([x[oth], x[own]], axis=0)
    d = {}
    d["x_in"] = np.ascontiguousarray(xo.T.reshape(FC, P, S))
    d["l0_wq"] = np.ascontiguousarray(
        np.asarray(ins["L0_attn_Wq"]).T.reshape(FC, P, HD))
    d["l0_wk"] = np.ascontiguousarray(
        np.asarray(ins["L0_attn_Wk"]).T.reshape(FC, P, HD))
    d["l0_wv"] = np.ascontiguousarray(
        np.asarray(ins["L0_attn_Wv"]).T.reshape(FC, P, HD))
    d["l0_wo"] = np.ascontiguousarray(np.asarray(ins["L0_attn_Wo"]).T)
    d["l0_wg"] = np.ascontiguousarray(
        np.asarray(ins["L0_ffn_w_gate"]).T.reshape(FC, P, H0))
    d["l0_wu"] = np.ascontiguousarray(
        np.asarray(ins["L0_ffn_w_up"]).T.reshape(FC, P, H0))
    d["l0_wd"] = np.ascontiguousarray(
        np.asarray(ins["L0_ffn_w_down"]).T.reshape(H0 // P, P, D))
    d["l1_wkv"] = np.ascontiguousarray(
        np.asarray(ins["L1_attn_W_kv_down"]).reshape(FC, P, DC))
    d["l1_wkup"] = np.ascontiguousarray(np.asarray(ins["L1_attn_W_k_up"]))
    d["l1_wvup"] = np.ascontiguousarray(np.asarray(ins["L1_attn_W_v_up"]))
    d["l1_wq"] = np.ascontiguousarray(
        np.asarray(ins["L1_attn_W_q"]).reshape(FC, P, D))
    wqr = np.asarray(ins["L1_attn_W_q_rope"])
    d["l1_wqr1"] = np.ascontiguousarray(
        np.concatenate([wqr[:, :16], wqr[:, :16]], 1).reshape(FC, P, DR))
    d["l1_wqr2"] = np.ascontiguousarray(
        np.concatenate([wqr[:, 16:], wqr[:, 16:]], 1).reshape(FC, P, DR))
    wkr = np.asarray(ins["L1_attn_W_k_rope"])
    d["l1_wkr1"] = np.ascontiguousarray(np.concatenate([wkr[:, :16], wkr[:, :16]], 1))
    d["l1_wkr2"] = np.ascontiguousarray(np.concatenate([wkr[:, 16:], wkr[:, 16:]], 1))
    wo = np.asarray(ins["L1_attn_W_o"])
    d["l1_wo"] = np.ascontiguousarray(
        wo.reshape(NH, HD, D).transpose(1, 0, 2).reshape(HD, NH * D))
    d["l1_wg"] = np.ascontiguousarray(
        np.asarray(ins["L1_ffn_w_gate"]).reshape(NHC1, P, FC, P)
        .transpose(0, 3, 2, 1).reshape(NHC1, P, D))
    d["l1_wu"] = np.ascontiguousarray(
        np.asarray(ins["L1_ffn_w_up"]).reshape(NHC1, P, FC, P)
        .transpose(0, 3, 2, 1).reshape(NHC1, P, D))
    d["l1_wd"] = np.ascontiguousarray(
        np.asarray(ins["L1_ffn_w_down"]).reshape(FC, P, NHC1, P)
        .transpose(2, 3, 0, 1).reshape(NHC1, P, D))

    km = np.ones((HD, NSLOT), np.float32)
    if h == 0:
        km[:, :16] = 0.0
    d["kfmask"] = km
    kb = np.full((1, S), -16.0, np.float32)
    if h == 0:
        kb[0, :TOWN] = -30000.0
    d["keybias"] = kb
    r = np.arange(P)
    d["ut_mask"] = (r[:, None] <= r[None, :]).astype(np.float32)
    dm = np.zeros((4, P, CH), np.float32)
    q = np.arange(CH)
    for rr in range(4):
        dm[rr] = (q[None, :] >= (rr * P + r[:, None])).astype(np.float32)
    d["dmask"] = dm
    inv = (1.0 / 10000.0 ** (np.arange(0, DR, 2) / DR)).astype(np.float64)
    pos_q = (h * TOWN + np.arange(TOWN)).astype(np.float64)
    pos_k = np.concatenate([(1 - h) * TOWN + np.arange(TOWN),
                            h * TOWN + np.arange(TOWN)]).astype(np.float64)

    def cstab(pos):
        fr = np.outer(pos, inv)
        c = np.cos(fr).astype(np.float32).T
        s = np.sin(fr).astype(np.float32).T
        return (np.ascontiguousarray(np.concatenate([c, s], 0)),
                np.ascontiguousarray(np.concatenate([-s, c], 0)))

    d["cs1q"], d["cs2q"] = cstab(pos_q)
    d["cs1k"], d["cs2k"] = cstab(pos_k)
    d["identf"] = np.eye(P, dtype=np.float32)
    return d


_NC_CACHE = {}


def kernel(**inputs):
    if "nc" not in _NC_CACHE:
        _NC_CACHE["nc"] = build_nc(debug=False)
    nc = _NC_CACHE["nc"]
    in_maps = [_host_inputs(inputs, c) for c in range(8)]
    res = run_bass_kernel_spmd(nc, in_maps, core_ids=list(range(8)))
    out = np.empty((4, S, D), np.float32)
    for c in range(8):
        b, h = c // 2, c % 2
        y = np.asarray(res.results[c]["y"])
        out[b, h * TOWN:(h + 1) * TOWN, :] = y.reshape(D, TOWN).T
    return out
